# version 01
# baseline (speedup 1.0000x reference)
import numpy as np
import ml_dtypes
from contextlib import ExitStack

import concourse.bass as bass
import concourse.mybir as mybir
from concourse.bass_utils import run_bass_kernel_spmd

F32 = mybir.dt.float32
BF16 = mybir.dt.bfloat16
U8 = mybir.dt.uint8
AF = mybir.ActivationFunctionType
ALU = mybir.AluOpType
AX = mybir.AxisListType.X
BFNP = ml_dtypes.bfloat16

B, T, D = 4, 4096, 1024
TH = T // 2
H, DH = 16, 64
HPC = H // 2
I_FF = 2816
D_INNER, D_STATE, NH, HD, DCONV = 2048, 128, 32, 64, 4
QMAX = 127.0
MAGIC = float(1.5 * 2 ** 23)
EPS = 1e-5
NCORES = 8
DC = D // 128
TM = T // 512
TMH = TH // 512


_DETECT_RACES = False
P2_NH = 8
P2_H0 = 0
P3_STAGE = 5
P3_SUB = 3
P3_DBG = False
P2_NHALF = 2


class Prog:
    def __init__(self):
        self.nc = bass.Bass(target_bir_lowering=False, debug=False,
                            detect_race_conditions=_DETECT_RACES)
        self.es = ExitStack()
        self.block = None
        self.cnt = {}

    def sb(self, name, shape, dt=BF16):
        return self.es.enter_context(self.nc.sbuf_tensor(name, list(shape), dt))

    def ps(self, name, shape, dt=F32):
        return self.es.enter_context(self.nc.psum_tensor(name, list(shape), dt))

    def sem(self, name):
        s = self.es.enter_context(self.nc.semaphore(name))
        self.cnt[name] = 0
        return s

    def din(self, name, shape, dt=F32):
        return self.nc.dram_tensor(name, list(shape), dt, kind="ExternalInput").ap()

    def dout(self, name, shape, dt=F32):
        return self.nc.dram_tensor(name, list(shape), dt, kind="ExternalOutput").ap()

    def dint(self, name, shape, dt=F32):
        return self.nc.dram_tensor(name, list(shape), dt, kind="Internal").ap()

    def start(self):
        self.block = self.es.enter_context(self.nc.Block())
        return self.block

    def inc(self, instr, sem, n=1):
        instr.then_inc(sem, n)
        self.cnt[sem.name] += n

    def incd(self, instr, sem):
        self.inc(instr, sem, 16)

    def wait(self, eng, sem):
        eng.wait_ge(sem, self.cnt[sem.name])

    def run(self, in_maps, **kw):
        res = run_bass_kernel_spmd(self.nc, in_maps, core_ids=list(range(NCORES)), **kw)
        return res



def build_p1():
    p = Prog()
    x_in = p.din("x_fm", [D, T])
    wqT_in = p.din("wqT", [D, 512])
    wkT_in = p.din("wkT", [D, 512])
    wvT_in = p.din("wvT", [D, 512])
    gam_in = p.din("gam8", [8, 128])
    bet_in = p.din("bet8", [8, 128])
    eps_in = p.din("eps11", [1, 1])

    xn_st = p.dout("xn_st", [D, T], BF16)
    xmax_st = p.dout("xmax_st", [1, 128])
    q_st = p.dout("q_st", [512, T], BF16)
    k_st = p.dout("k_st", [512, T], BF16)
    v_st = p.dout("v_st", [T, 512], BF16)

    x_bf = p.sb("x_bf", [128, DC * T])
    wT = p.sb("wT", [128, DC * 1536])
    gam_t = p.sb("gam_t", [128, 8], F32)
    bet_t = p.sb("bet_t", [128, 8])
    sqt = p.sb("sqt", [128, 2 * 512])
    onescol = p.sb("onescol", [128, 1])
    onesrow = p.sb("onesrow", [1, 512])
    r_mean = p.sb("r_mean", [1, T], F32)
    r_var = p.sb("r_var", [1, T], F32)
    r_rstd = p.sb("r_rstd", [1, 512], F32)
    r_tmp = p.sb("r_tmp", [1, 512], F32)
    r_tmp2 = p.sb("r_tmp2", [1, 512], F32)
    r_s2 = p.sb("r_s2", [1, 512], F32)
    rb_sx = p.sb("rb_sx", [1, 512])
    rb_ox = p.sb("rb_ox", [1, 512])
    rb_s2 = p.sb("rb_s2", [1, T])
    eps_t = p.sb("eps_t", [1, 1], F32)
    bct = p.sb("bct", [128, 2 * 1024])
    s2bc = p.sb("s2bc", [128, T])
    xmax_c = p.sb("xmax_c", [128, 1], F32)
    xmax_buf = p.sb("xmax_buf", [128, TM * DC], F32)
    junk8 = p.sb("junk8", [128, 8], F32)
    tmax_c = p.sb("tmax_c", [128, 1], F32)
    bqk_c = p.sb("bqk_c", [128, 8], F32)
    bv_row = p.sb("bv_row", [1, 512])
    bv_bc = p.sb("bv_bc", [128, 512])
    s2col = p.sb("s2col", [128, 32], F32)
    ev = p.sb("ev", [128, 2 * 512])

    p_sum = p.ps("p_sum", [1, 512])
    p_sq = p.ps("p_sq", [1, 512])
    p_bc = p.ps("p_bc", [128, 512])
    p_bcol = p.ps("p_bcol", [128, 1])
    p_brow = p.ps("p_brow", [1, 512])
    p_q = [p.ps("p_qA", [128, 512]), p.ps("p_qB", [128, 512])]
    p_tp = p.ps("p_tp", [128, 1], BF16)

    ld = p.sem("ld")
    lds = p.sem("lds")
    sqd = p.sem("sqd")
    tmm = p.sem("tmm")
    smm = p.sem("smm")
    sev = p.sem("sev")
    rows = p.sem("rows")
    rowsb = p.sem("rowsb")
    bcmm = p.sem("bcmm")
    bcev = p.sem("bcev")
    xnd = p.sem("xnd")
    xmd = p.sem("xmd")
    s2mm = p.sem("s2mm")
    s2ev = p.sem("s2ev")
    bmm = p.sem("bmm")
    bev = p.sem("bev")
    wf = p.sem("wf")
    qmm = p.sem("qmm")
    qev = p.sem("qev")
    st = p.sem("st")
    block = p.start()

    @block.gpsimd
    def _(g):
        for i in range(DC):
            p.incd(g.dma_start(out=x_bf[:, i * T:(i + 1) * T], in_=x_in[i * 128:(i + 1) * 128, :]), ld)
        for i in range(DC):
            p.incd(g.dma_start(out=wT[:, i * 1536:i * 1536 + 512], in_=wqT_in[i * 128:(i + 1) * 128, :]), ld)
            p.incd(g.dma_start(out=wT[:, i * 1536 + 512:i * 1536 + 1024], in_=wkT_in[i * 128:(i + 1) * 128, :]), ld)
            p.incd(g.dma_start(out=wT[:, i * 1536 + 1024:i * 1536 + 1536], in_=wvT_in[i * 128:(i + 1) * 128, :]), ld)
        for i in range(8):
            p.incd(g.dma_start(out=bet_t[:, i:i + 1], in_=bet_in[i:i + 1, :].transpose([1, 0])), ld)
        g.memset(onescol[:], 1.0)
        g.memset(onesrow[:], 1.0)
        g.memset(xmax_c[:], 0.0)
        p.inc(g.engine_nop(), ld)

    @block.sync
    def _(s):
        for i in range(8):
            p.incd(s.dma_start(out=gam_t[:, i:i + 1], in_=gam_in[i:i + 1, :].transpose([1, 0])), lds)
        p.incd(s.dma_start(out=eps_t[:], in_=eps_in[:, :]), lds)

    @block.scalar
    def _(sc):
        p.wait(sc, ld)
        for k in range(TM * DC):
            m, i = divmod(k, DC)
            if k >= 2:
                sc.wait_ge(tmm, k)
            p.inc(sc.activation(sqt[:, (k % 2) * 512:(k % 2 + 1) * 512],
                                x_bf[:, i * T + m * 512: i * T + (m + 1) * 512],
                                AF.Square), sqd)

    @block.tensor
    def _(t):
        for m in range(TM):
            if m > 0:
                t.wait_ge(sev, m)
            for i in range(DC):
                k = m * DC + i
                t.wait_ge(sqd, k + 1)
                ins0 = t.matmul(p_sum[:], onescol[:], x_bf[:, i * T + m * 512: i * T + (m + 1) * 512],
                                start=(i == 0), stop=(i == DC - 1))
                p.inc(ins0, tmm)
                ins = t.matmul(p_sq[:], onescol[:], sqt[:, (k % 2) * 512:(k % 2 + 1) * 512],
                               start=(i == 0), stop=(i == DC - 1))
                if i == DC - 1:
                    p.inc(ins, smm)

    @block.vector
    def _(v):
        for m in range(TM):
            v.wait_ge(smm, m + 1)
            mm = r_mean[0:1, m * 512:(m + 1) * 512]
            vv = r_var[0:1, m * 512:(m + 1) * 512]
            v.tensor_scalar_mul(mm, p_sum[:], 1.0 / D)
            v.tensor_scalar_mul(vv, p_sq[:], 1.0 / D)
            v.tensor_mul(r_tmp2[:], mm, mm)
            p.inc(v.tensor_sub(vv, vv, r_tmp2[:]), sev)

    for m in range(TM):
        mm_ = lambda row: row[0:1, m * 512:(m + 1) * 512]

        @block.scalar
        def _(sc, m=m):
            sc.wait_ge(sev, m + 1)
            if m == 0:
                p.wait(sc, lds)
            sc.activation(r_tmp[:], r_var[0:1, m * 512:(m + 1) * 512], AF.Ln, bias=eps_t[:])
            p.inc(sc.activation(r_rstd[:], r_tmp[:], AF.Exp, scale=-0.5), rows)

        @block.vector
        def _(v, m=m):
            v.wait_ge(rows, 4 * m + 1)
            v.tensor_mul(r_tmp[:], r_rstd[:], r_rstd[:])
            p.inc(v.tensor_mul(r_tmp[:], r_tmp[:], r_var[0:1, m * 512:(m + 1) * 512]), rows)

        @block.scalar
        def _(sc, m=m):
            sc.wait_ge(rows, 4 * m + 2)
            sc.activation(r_s2[:], r_tmp[:], AF.Ln, bias=eps_t[:])
            p.inc(sc.activation(r_s2[:], r_s2[:], AF.Exp, scale=-0.5), rows)

        @block.vector
        def _(v, m=m):
            v.wait_ge(rows, 4 * m + 3)
            v.tensor_mul(r_tmp[:], r_mean[0:1, m * 512:(m + 1) * 512], r_rstd[:])
            p.inc(v.tensor_scalar_mul(r_tmp[:], r_tmp[:], -1.0), rows)

        @block.scalar
        def _(sc, m=m):
            sc.wait_ge(rows, 4 * m + 4)
            sc.activation(rb_sx[:], r_rstd[:], AF.Copy)
            sc.activation(rb_ox[:], r_tmp[:], AF.Copy)
            p.inc(sc.activation(rb_s2[0:1, m * 512:(m + 1) * 512], r_s2[:], AF.Copy), rowsb)

        @block.tensor
        def _(t, m=m):
            t.wait_ge(rowsb, m + 1)
            if m >= 2:
                t.wait_ge(bcev, 3 * (m - 1))
            p.inc(t.matmul(p_bc[:], onesrow[0:1, 0:128], rb_sx[0:1, :], start=True, stop=True), bcmm)
            t.wait_ge(bcev, 3 * m + 1)
            p.inc(t.matmul(p_bc[:], onesrow[0:1, 0:128], rb_ox[0:1, :], start=True, stop=True), bcmm)
            t.wait_ge(bcev, 3 * m + 2)
            p.inc(t.matmul(p_bc[:], onesrow[0:1, 0:128], rb_s2[0:1, m * 512:(m + 1) * 512], start=True, stop=True), bcmm)

        @block.vector
        def _(v, m=m):
            q = m % 2
            v.wait_ge(bcmm, 3 * m + 1)
            p.inc(v.tensor_copy(bct[:, q * 1024:q * 1024 + 512], p_bc[:]), bcev)
            v.wait_ge(bcmm, 3 * m + 2)
            p.inc(v.tensor_copy(bct[:, q * 1024 + 512:q * 1024 + 1024], p_bc[:]), bcev)
            v.wait_ge(bcmm, 3 * m + 3)
            p.inc(v.tensor_copy(s2bc[:, m * 512:(m + 1) * 512], p_bc[:]), bcev)

        @block.vector
        def _(v, m=m):
            q = m % 2
            v.wait_ge(bcev, 3 * m + 2)
            if m == 0:
                v.wait_ge(sqd, TM * DC)
            for i in range(DC):
                xs = x_bf[:, i * T + m * 512: i * T + (m + 1) * 512]
                v.tensor_mul(xs, xs, bct[:, q * 1024:q * 1024 + 512])
                v.tensor_add(xs, xs, bct[:, q * 1024 + 512:q * 1024 + 1024])
                p.inc(v.reduce_max(xmax_buf[:, (m * DC + i):(m * DC + i) + 1], xs,
                                   axis=AX, apply_absolute_value=True), xnd)


    @block.scalar
    def _(sc):
        p.wait(sc, xnd)
        p.inc(sc.activation(junk8[:], xmax_buf[:, TM * DC - 8:TM * DC], AF.Copy), xmd)

    @block.vector
    def _(v):
        v.wait_ge(xmd, 1)
        p.inc(v.reduce_max(xmax_c[:, 0:1], xmax_buf[:, 0:TM * DC],
                           axis=AX, apply_absolute_value=True), xmd)

    @block.sync
    def _(s):
        for m in range(TM):
            for i in range(DC):
                s.wait_ge(xnd, m * DC + i + 1)
                p.incd(s.dma_start(out=xn_st[i * 128:(i + 1) * 128, m * 512:(m + 1) * 512],
                                   in_=x_bf[:, i * T + m * 512: i * T + (m + 1) * 512]), st)
        p.wait(s, xmd)
        p.incd(s.dma_start(out=xmax_st[0:1, :], in_=xmax_c[:, 0:1]), st)

    @block.tensor
    def _(t):
        p.wait(t, rowsb)
        for tt in range(32):
            if tt >= 1:
                t.wait_ge(s2ev, tt)
            p.inc(t.transpose(p_tp[:, 0:1], rb_s2[0:1, tt * 128:(tt + 1) * 128], onesrow[0:1, 0:1]), s2mm)

    @block.vector
    def _(v):
        for tt in range(32):
            v.wait_ge(s2mm, tt + 1)
            p.inc(v.tensor_copy(s2col[:, tt:tt + 1], p_tp[:, 0:1]), s2ev)

    @block.tensor
    def _(t):
        p.wait(t, ld)
        for o in range(8):
            proj, oc = divmod(o, 4)
            for i in range(DC):
                ins = t.matmul(p_bcol[:, 0:1],
                               wT[:, i * 1536 + proj * 512 + oc * 128: i * 1536 + proj * 512 + (oc + 1) * 128],
                               bet_t[:, i:i + 1], start=(i == 0), stop=(i == DC - 1))
            p.inc(ins, bmm)
            t.wait_ge(bev, o + 1)
        for i in range(DC):
            ins = t.matmul(p_brow[0:1, :], bet_t[:, i:i + 1], wT[:, i * 1536 + 1024: i * 1536 + 1536],
                           start=(i == 0), stop=(i == DC - 1))
        p.inc(ins, bmm)
        t.wait_ge(bev, 9)
        p.inc(t.matmul(p_bc[:], onesrow[0:1, 0:128], bv_row[0:1, :], start=True, stop=True), bmm)

    @block.vector
    def _(v):
        for o in range(8):
            v.wait_ge(bmm, o + 1)
            p.inc(v.tensor_copy(bqk_c[:, o:o + 1], p_bcol[:, 0:1]), bev)
        v.wait_ge(bmm, 9)
        p.inc(v.tensor_copy(bv_row[0:1, :], p_brow[0:1, :]), bev)
        v.wait_ge(bmm, 10)
        p.inc(v.tensor_copy(bv_bc[:], p_bc[:]), bev)

    @block.gpsimd
    def _(g):
        p.wait(g, bmm)
        p.wait(g, lds)
        for i in range(DC):
            for pr in range(3):
                ins = g.tensor_scalar_mul(wT[:, i * 1536 + pr * 512:i * 1536 + (pr + 1) * 512],
                                          wT[:, i * 1536 + pr * 512:i * 1536 + (pr + 1) * 512],
                                          gam_t[:, i:i + 1])
        p.inc(ins, wf)

    @block.tensor
    def _(t):
        p.wait(t, wf)
        p.wait(t, xnd)
        p.wait(t, bcev)
        nt = 0
        for m in range(TM):
            for o in range(8):
                proj, oc = divmod(o, 4)
                pq = p_q[nt % 2]
                if nt >= 2:
                    t.wait_ge(qev, nt - 1)
                for i in range(DC):
                    ins = t.matmul(pq[:],
                                   wT[:, i * 1536 + proj * 512 + oc * 128: i * 1536 + proj * 512 + (oc + 1) * 128],
                                   x_bf[:, i * T + m * 512:i * T + (m + 1) * 512],
                                   start=(i == 0), stop=(i == DC - 1))
                p.inc(ins, qmm)
                nt += 1

    qst = [p.sem("qstA"), p.sem("qstB")]

    @block.vector
    def _(v):
        nt = 0
        for m in range(TM):
            for o in range(8):
                pq = p_q[nt % 2]
                v.wait_ge(qmm, nt + 1)
                if nt >= 2:
                    v.wait_ge(qst[nt % 2], 16 * (nt // 2))
                e = ev[:, (nt % 2) * 512:(nt % 2 + 1) * 512]
                v.tensor_mul(e, pq[:], s2bc[:, m * 512:(m + 1) * 512])
                p.inc(v.tensor_scalar_add(e, e, bqk_c[:, o:o + 1]), qev)
                nt += 1

    @block.sync
    def _(s):
        nt = 0
        for m in range(TM):
            for o in range(8):
                proj, oc = divmod(o, 4)
                s.wait_ge(qev, nt + 1)
                if nt >= 2:
                    s.wait_ge(qst[nt % 2], 16 * (nt // 2))
                dst = q_st if proj == 0 else k_st
                p.incd(s.dma_start(out=dst[oc * 128:(oc + 1) * 128, m * 512:(m + 1) * 512],
                                   in_=ev[:, (nt % 2) * 512:(nt % 2 + 1) * 512]), qst[nt % 2])
                nt += 1

    @block.tensor
    def _(t):
        t.wait_ge(bev, 10)
        nt = 64
        for tt in range(32):
            pq = p_q[nt % 2]
            t.wait_ge(qev, nt - 1)
            for i in range(DC):
                t.matmul(pq[:], x_bf[:, i * T + tt * 128:i * T + (tt + 1) * 128],
                         wT[:, i * 1536 + 1024:i * 1536 + 1536],
                         start=(i == 0), stop=False)
            ins = t.matmul(pq[:], onesrow[0:1, 0:128], bv_row[0:1, :], start=False, stop=True)
            p.inc(ins, qmm)
            nt += 1

    @block.vector
    def _(v):
        nt = 64
        for tt in range(32):
            pq = p_q[nt % 2]
            v.wait_ge(qmm, nt + 1)
            v.wait_ge(qst[nt % 2], 16 * (nt // 2))
            e = ev[:, (nt % 2) * 512:(nt % 2 + 1) * 512]
            v.tensor_sub(e, pq[:], bv_bc[:])
            v.tensor_scalar_mul(e, e, s2col[:, tt:tt + 1])
            p.inc(v.tensor_add(e, e, bv_bc[:]), qev)
            nt += 1

    @block.sync
    def _(s):
        nt = 64
        for tt in range(32):
            s.wait_ge(qev, nt + 1)
            s.wait_ge(qst[nt % 2], 16 * (nt // 2))
            p.incd(s.dma_start(out=v_st[tt * 128:(tt + 1) * 128, :],
                               in_=ev[:, (nt % 2) * 512:(nt % 2 + 1) * 512]), qst[nt % 2])
            nt += 1
        p.wait(s, st)
        p.wait(s, qst[0])
        p.wait(s, qst[1])

    return p



def _p2_structure():
    p1 = [(qs, km) for qs in range(32) for km in range(qs // 4 + 1)]
    halves = []
    for half in range(2):
        qms = list(range(4 * half, 4 * half + 4))
        ktmax = 4 * qms[-1] + 3
        tiles = []
        for kt in range(ktmax + 1):
            for qm in qms:
                if qm >= (kt // 4) if True else False:
                    pass
                if qm >= kt // 4:
                    tiles.append((kt, qm))
        halves.append((qms, ktmax, tiles))
    return p1, halves


def build_p2():
    p = Prog()
    q_in = p.din("q_st", [512, T], BF16)
    k_in = p.din("k_st", [512, T], BF16)
    v_in = p.din("v_st", [T, 512], BF16)
    triq_in = p.din("triq", [128, 128], U8)
    trik_in = p.din("trik", [128, 128], U8)
    id_in = p.din("ident_f", [128, 128])
    ctx_st = p.dout("ctx_st", [512, T], BF16)
    mstage = p.dint("mstage", [32, 128], BF16)

    qp = p.sb("qp", [128, 4 * T])
    kp = p.sb("kp", [128, 4 * T])
    v65 = p.sb("v65", [128, 32 * 520])
    qscr = p.sb("qscr", [128, 2 * T])
    kscr = p.sb("kscr", [128, 2 * 128])
    pt = p.sb("pt", [128, 2 * 512])
    triq_m = p.sb("triq_m", [128, 128], U8)
    trik_m = p.sb("trik_m", [128, 128], U8)
    neginf = p.sb("neginf", [128, 128], F32)
    selbuf = p.sb("selbuf", [128, 128], F32)
    selstrip = p.sb("selstrip", [128, 2 * 128], F32)
    m_all = p.sb("m_all", [128, 32], F32)
    m_bf = p.sb("m_bf", [128, 32])
    tcol = p.sb("tcol", [128, 1], F32)
    negm_row = p.sb("negm_row", [32, 128])
    lr_row = p.sb("lr_row", [1, 512], F32)
    lr_bf = p.sb("lr_bf", [1, 512])
    lrbc = p.sb("lrbc", [128, 512])
    onesrow = p.sb("onesrow", [1, 128])
    id_bf = p.sb("id_bf", [128, 128])
    ev = p.sb("ev", [128, 2 * 512])

    p_s = [p.ps("p_sA", [128, 512]), p.ps("p_sB", [128, 512])]
    p_ctx = [p.ps(f"p_ctx{i}", [65, 512]) for i in range(4)]
    p_lb = p.ps("p_lb", [128, 512])
    p_mt = p.ps("p_mt", [32, 128], BF16)

    ld = p.sem("ld")
    ldg = p.sem("ldg")
    msd = p.sem("msd")
    pmm = p.sem("pmm")
    sfree = p.sem("sfree")
    qcp = p.sem("qcp")
    mtr = p.sem("mtr")
    mbo = p.sem("mbo")
    mbi = p.sem("mbi")
    kcp = p.sem("kcp")
    smm = p.sem("smm")
    msk = p.sem("msk")
    cmf = p.sem("cmf")
    lrc = p.sem("lrc")
    lct = p.sem("lct")
    lbc = p.sem("lbc")
    lev = p.sem("lev")
    cev = p.sem("cev")
    st = p.sem("st")
    cst = [p.sem("cstA"), p.sem("cstB")]
    block = p.start()

    p1_list, halves = _p2_structure()
    NP1 = len(p1_list)
    NT_half = [len(halves[0][2]), len(halves[1][2])]
    NT = NT_half[0] + (NT_half[1] if P2_NHALF == 2 else 0)

    cumtiles = {}
    for half, (qms, ktmax, tiles) in enumerate(halves):
        c = 0
        percnt = {}
        for kt in range(ktmax + 1):
            c += sum(1 for (k2, _) in tiles if k2 == kt)
            percnt[kt] = c
        cumtiles[half] = percnt

    @block.gpsimd
    def _(g):
        g.memset(v65[:], 1.0)
        p.inc(g.memset(onesrow[:], 1.0), msd)
        g.wait_ge(msd, 1)
        p.incd(g.dma_start(out=id_bf[:], in_=id_in[:, :]), ldg)
        for kt in range(32):
            src = v_in[kt * 128:(kt + 1) * 128, :].rearrange("p (h e) -> p h e", h=8)
            dst = v65[:, kt * 520:(kt + 1) * 520].rearrange("p (h e) -> p h e", h=8)[:, :, 0:64]
            p.incd(g.dma_start(out=dst, in_=src), ldg)

    @block.sync
    def _(s):
        for j in range(4):
            p.incd(s.dma_start(out=qp[:, j * T:(j + 1) * T], in_=q_in[j * 128:(j + 1) * 128, :]), ld)
            p.incd(s.dma_start(out=kp[:, j * T:(j + 1) * T], in_=k_in[j * 128:(j + 1) * 128, :]), ld)
        p.incd(s.dma_start(out=triq_m[:], in_=triq_in[:, :]), ld)
        p.incd(s.dma_start(out=trik_m[:], in_=trik_in[:, :]), ld)

    @block.vector
    def _(v):
        v.memset(kscr[64:65, :], 1.0)
        v.memset(neginf[:], -1e30)
        p.inc(v.engine_nop(), kcp)

    nS = 0
    exp_sfree = {}
    stop_cmf = {}
    diag_order = []
    for _h in range(P2_NH - P2_H0):
        for _half in range(P2_NHALF):
            _tb = _h * NT + (NT_half[0] if _half == 1 else 0)
            for _i, (_kt, _qm) in enumerate(halves[_half][2]):
                if _kt >= 4 * _qm:
                    diag_order.append(_tb + _i)
    diag_slot = {gi: (d % 2) for d, gi in enumerate(diag_order)}
    diag_prev = {gi: (diag_order[d - 2] if d >= 2 else None) for d, gi in enumerate(diag_order)}
    ALLSEMS = None
    for h in range(P2_H0, P2_NH):
        j, r = divmod(h, 2)
        base = 64 * r
        slot = h % 2
        p1base_mm = (h - P2_H0) * NP1
        smmbase = (h - P2_H0) * NT
        tilebase = (h - P2_H0) * NT

        if h > 0:
            allsems = [pmm, sfree, qcp, mtr, mbo, mbi, kcp, smm, msk, cmf, lrc, lct, lbc, lev, cev, st]

            @block.tensor
            def _(t):
                for sm in allsems:
                    p.wait(t, sm)

            @block.vector
            def _(v):
                for sm in allsems:
                    p.wait(v, sm)

            @block.scalar
            def _(sc):
                for sm in allsems:
                    p.wait(sc, sm)

            @block.gpsimd
            def _(g):
                for sm in allsems:
                    p.wait(g, sm)

        @block.tensor
        def _(t, h=h, j=j, base=base, nS0=nS):
            if h == P2_H0:
                p.wait(t, ld)
                p.wait(t, ldg)
                p.wait(t, kcp)
            s = nS0
            for (qs, km) in p1_list:
                diag = (km == qs // 4)
                width = 128 * (qs % 4) + 128 if diag else 512
                if s >= 2:
                    t.wait_ge(sfree, s - 1)
                p.inc(t.matmul(p_s[s % 2][:, 0:width],
                               qp[base:base + 64, j * T + qs * 128:j * T + (qs + 1) * 128],
                               kp[base:base + 64, j * T + km * 512:j * T + km * 512 + width],
                               start=True, stop=True), pmm)
                s += 1

        @block.vector
        def _(v, h=h, nS0=nS, p1base_mm=p1base_mm):
            s = nS0
            for i, (qs, km) in enumerate(p1_list):
                diag = (km == qs // 4)
                R = 128 * (qs % 4)
                v.wait_ge(pmm, p1base_mm + i + 1)
                first = (km == 0)
                ops = []
                if not diag:
                    ops.append(("red", (0, 512)))
                else:
                    if R > 0:
                        ops.append(("red", (0, R)))
                    ops.append(("sel", (R, R + 128)))
                last = None
                for kind, (a, b) in ops:
                    if kind == "red":
                        if first:
                            last = v.reduce_max(m_all[:, qs:qs + 1], p_s[s % 2][:, a:b], axis=AX)
                            first = False
                        else:
                            v.reduce_max(tcol[:], p_s[s % 2][:, a:b], axis=AX)
                            last = v.tensor_max(m_all[:, qs:qs + 1], m_all[:, qs:qs + 1], tcol[:])
                    else:
                        v.select(selbuf[:], triq_m[:], p_s[s % 2][:, a:b], neginf[:])
                        if first:
                            last = v.reduce_max(m_all[:, qs:qs + 1], selbuf[:], axis=AX)
                            first = False
                        else:
                            v.reduce_max(tcol[:], selbuf[:], axis=AX)
                            last = v.tensor_max(m_all[:, qs:qs + 1], m_all[:, qs:qs + 1], tcol[:])
                p.inc(last, sfree)
                s += 1
        nS += NP1

        @block.vector
        def _(v, h=h, slot=slot, j=j, base=base):
            p.wait(v, sfree)
            hh = h - P2_H0
            v.wait_ge(mbo, 16 * hh)
            if hh >= 1:
                v.wait_ge(mtr, 3 * hh - 1)
            p.inc(v.tensor_copy(m_bf[:], m_all[:]), mtr)
            if hh >= 2:
                v.wait_ge(smm, (hh - 1) * NT)
            p.inc(v.tensor_copy(qscr[0:64, slot * T:(slot + 1) * T],
                                qp[base:base + 64, j * T:(j + 1) * T]), qcp)

        @block.tensor
        def _(t, h=h):
            t.wait_ge(mtr, 3 * (h - P2_H0) + 1)
            p.inc(t.transpose(p_mt[:], m_bf[:], id_bf[:]), mtr)

        @block.vector
        def _(v, h=h):
            v.wait_ge(mtr, 3 * (h - P2_H0) + 2)
            p.inc(v.tensor_scalar_mul(negm_row[:], p_mt[:], -1.0), mtr)

        @block.gpsimd
        def _(g, h=h, slot=slot):
            hh = h - P2_H0
            g.wait_ge(mtr, 3 * hh + 3)
            if hh >= 1:
                g.wait_ge(mbi, 16 * hh)
            p.incd(g.dma_start(out=mstage[:, :], in_=negm_row[:]), mbo)
            g.wait_ge(mbo, 16 * (hh + 1))
            if hh >= 2:
                g.wait_ge(smm, (hh - 1) * NT)
            p.incd(g.dma_start(out=qscr[64:65, slot * T:(slot + 1) * T],
                               in_=mstage.rearrange("a b -> (a b)").rearrange("(o c) -> o c", o=1)), mbi)

        for half in range(P2_NHALF):
            qms, ktmax, tiles = halves[half]
            kcpbase = p.cnt["kcp"]


            mskmap = {}
            _mc = p.cnt["msk"]
            for (kt, qm) in halves[half][2]:
                if kt >= 4 * qm:
                    _mc += 1
                    mskmap[(kt, qm)] = _mc

            @block.scalar
            def _(sc, h=h, half=half, smmbase=smmbase, nS0=nS, tiles=halves[half][2], tilebase=tilebase):
                pre = smmbase + (NT_half[0] if half == 1 else 0)
                tb = tilebase + (NT_half[0] if half == 1 else 0)
                s = nS0
                for i, (kt, qm) in enumerate(tiles):
                    gi = tb + i
                    r = kt - 4 * qm
                    c0 = 128 * r if r >= 0 else 0
                    if gi >= 2:
                        sc.wait_ge(cmf, gi - 1)
                    if r >= 0:
                        sc.wait_ge(msk, mskmap[(kt, qm)])
                        slot = diag_slot[gi]
                        ins = sc.activation(pt[:, (gi % 2) * 512 + c0:(gi % 2) * 512 + c0 + 128],
                                            selstrip[:, slot * 128:(slot + 1) * 128],
                                            AF.Exp, scale=0.125)
                        if c0 + 128 < 512:
                            sc.wait_ge(smm, pre + i + 1)
                            ins = sc.activation(pt[:, (gi % 2) * 512 + c0 + 128:(gi % 2 + 1) * 512],
                                                p_s[s % 2][:, 128:512 - c0],
                                                AF.Exp, scale=0.125)
                    else:
                        sc.wait_ge(smm, pre + i + 1)
                        ins = sc.activation(pt[:, (gi % 2) * 512 + c0:(gi % 2 + 1) * 512],
                                            p_s[s % 2][:, 0:512 - c0],
                                            AF.Exp, scale=0.125)
                    p.inc(ins, sfree)
                    exp_sfree[gi] = p.cnt["sfree"]
                    s += 1

            @block.vector
            def _(v, h=h, half=half, j=j, base=base, ktmax=ktmax, smmbase=smmbase,
                  kcpbase=kcpbase, tiles=halves[half][2], tilebase=tilebase, nS0=nS):
                pre = smmbase + (NT_half[0] if half == 1 else 0)
                tb = tilebase + (NT_half[0] if half == 1 else 0)

                def emit_copy(kt):
                    if kt >= 2:
                        v.wait_ge(smm, pre + cumtiles[half][kt - 2])
                    elif pre > 0:
                        v.wait_ge(smm, pre)
                    p.inc(v.tensor_copy(kscr[0:64, (kt % 2) * 128:(kt % 2) * 128 + 128],
                                        kp[base:base + 64, j * T + kt * 128:j * T + (kt + 1) * 128]), kcp)

                emit_copy(0)
                if ktmax >= 1:
                    emit_copy(1)
                sbank = nS0
                for kt in range(ktmax + 1):
                    for i, (k2, qm) in enumerate(tiles):
                        if k2 != kt:
                            continue
                        gi = tb + i
                        r = kt - 4 * qm
                        if r >= 0:
                            v.wait_ge(smm, pre + i + 1)
                            slot = diag_slot[gi]
                            if diag_prev[gi] is not None:
                                v.wait_ge(sfree, exp_sfree[diag_prev[gi]])
                            p.inc(v.select(selstrip[:, slot * 128:(slot + 1) * 128],
                                           trik_m[:], p_s[(sbank + i) % 2][:, 0:128],
                                           neginf[:]), msk)
                    if kt + 2 <= ktmax:
                        emit_copy(kt + 2)

            def emit_ctx(t, i, kt, qm, tb):
                gi = tb + i
                r = kt - 4 * qm
                c0 = 128 * r if r >= 0 else 0
                t.wait_ge(sfree, exp_sfree[gi])
                if kt == 0:
                    cslot = (h - P2_H0) * 4 * P2_NHALF + (qm if P2_NHALF == 2 else qm % 4)
                    if cslot >= 4:
                        t.wait_ge(cev, cslot - 3)
                ins = t.matmul(p_ctx[qm % 4][:, c0:512],
                               v65[:, kt * 520 + h * 65: kt * 520 + (h + 1) * 65],
                               pt[:, (gi % 2) * 512 + c0:(gi % 2 + 1) * 512],
                               start=(kt == 0), stop=(kt == 4 * qm + 3))
                p.inc(ins, cmf)
                if kt == 4 * qm + 3:
                    stop_cmf[(h, qm)] = p.cnt["cmf"]

            @block.tensor
            def _(t, h=h, half=half, slot=slot, ktmax=ktmax, kcpbase=kcpbase, nS0=nS, tiles=halves[half][2], tilebase=tilebase):
                tb = tilebase + (NT_half[0] if half == 1 else 0)
                if half == 0:
                    p.wait(t, qcp)
                    p.wait(t, mbi)
                s = nS0
                for i, (kt, qm) in enumerate(tiles):
                    r = kt - 4 * qm
                    c0 = 128 * r if r >= 0 else 0
                    t.wait_ge(kcp, kcpbase + kt + 1)
                    if s >= 2:
                        t.wait_ge(sfree, s - 1)
                    p.inc(t.matmul(p_s[s % 2][:, 0:512 - c0],
                                   kscr[0:65, (kt % 2) * 128:(kt % 2) * 128 + 128],
                                   qscr[0:65, slot * T + qm * 512 + c0: slot * T + (qm + 1) * 512],
                                   start=True, stop=True), smm)
                    s += 1
                    if i >= 1:
                        kp_, qp_ = tiles[i - 1]
                        emit_ctx(t, i - 1, kp_, qp_, tb)
                emit_ctx(t, len(tiles) - 1, tiles[-1][0], tiles[-1][1], tb)

            nS += len(halves[half][2])

            for qm in qms:
                cslot = (h - P2_H0) * 4 * P2_NHALF + (qm if P2_NHALF == 2 else qm % 4)

                @block.vector
                def _(v, h=h, qm=qm, cslot=cslot):
                    v.wait_ge(cmf, stop_cmf[(h, qm)])
                    if cslot >= 1:
                        v.wait_ge(lct, cslot)
                    p.inc(v.reciprocal(lr_row[:], p_ctx[qm % 4][64:65, :]), lrc)

                @block.scalar
                def _(sc, h=h, qm=qm, cslot=cslot):
                    sc.wait_ge(lrc, cslot + 1)
                    if cslot >= 1:
                        sc.wait_ge(lbc, cslot)
                    p.inc(sc.activation(lr_bf[:], lr_row[:], AF.Copy), lct)

                @block.tensor
                def _(t, h=h, qm=qm, cslot=cslot):
                    t.wait_ge(lct, cslot + 1)
                    if cslot >= 1:
                        t.wait_ge(lev, cslot)
                    p.inc(t.matmul(p_lb[:], onesrow[0:1, 0:128], lr_bf[0:1, :], start=True, stop=True), lbc)

                @block.vector
                def _(v, h=h, qm=qm, cslot=cslot):
                    v.wait_ge(lbc, cslot + 1)
                    p.inc(v.tensor_copy(lrbc[:], p_lb[:]), lev)
                    if cslot >= 2:
                        v.wait_ge(cst[cslot % 2], 16 * (cslot // 2))
                    p.inc(v.tensor_mul(ev[0:64, (cslot % 2) * 512:(cslot % 2) * 512 + 512],
                                       p_ctx[qm % 4][0:64, :], lrbc[0:64, :]), cev)

                @block.sync
                def _(s, h=h, qm=qm, cslot=cslot):
                    s.wait_ge(cev, cslot + 1)
                    if cslot >= 2:
                        s.wait_ge(cst[cslot % 2], 16 * (cslot // 2))
                    p.incd(s.dma_start(out=ctx_st[h * 64:(h + 1) * 64, qm * 512:(qm + 1) * 512],
                                       in_=ev[0:64, (cslot % 2) * 512:(cslot % 2) * 512 + 512]), cst[cslot % 2])

    @block.sync
    def _(s):
        p.wait(s, st)
        p.wait(s, cst[0])
        p.wait(s, cst[1])

    return p



IC = I_FF // 128


def build_p3():
    p = Prog()
    xn_in = p.din("xn_own", [D, TH], BF16)
    ctx_in = p.din("ctx_own", [D, TH], BF16)
    parts_in = p.din("xmax_parts", [8, 128])
    clip_in = p.din("clip_x", [1, 1])
    wg_in = p.din("w_gate", [I_FF, D])
    wu_in = p.din("w_up", [I_FF, D])
    woT_in = p.din("W_oT", [D, D])
    id_in = p.din("ident_f", [128, 128])

    attn_st = p.dout("attn_st", [D, TH])
    h_st = p.dout("h_st", [I_FF, TH])
    hmax_st = p.dout("hmax_st", [1, 128])
    xq_dbg = p.dout("xq_dbg", [D, TH], BF16) if P3_DBG else None
    qwT_dbg = p.dout("qwT_dbg", [D, I_FF], BF16) if P3_DBG else None

    xq = p.sb("xq", [128, DC * TH])
    ctx_t = p.sb("ctx_t", [128, DC * TH])
    woT = p.sb("woT", [128, DC * 1024])
    qwT = p.sb("qwT", [128, 2 * DC * I_FF])
    wtile = p.sb("wtile", [128, 2 * 1024], F32)
    wq_t = p.sb("wq_t", [128, 1024])
    id_bf = p.sb("id_bf", [128, 128])
    parts_t = p.sb("parts_t", [1, 1024], F32)
    clip_t = p.sb("clip_t", [1, 1], F32)
    srow = p.sb("srow", [1, 2], F32)
    scl2 = p.sb("scl2", [128, 2], F32)
    wcol = p.sb("wcol", [128, 2], F32)
    junkc = p.sb("junkc", [128, 2], F32)
    wcol2 = p.sb("wcol2", [128, 1], F32)
    tmpf = p.sb("tmpf", [128, 2048], F32)
    hmax_c = p.sb("hmax_c", [128, 1], F32)
    hmax_buf = p.sb("hmax_buf", [128, TMH * IC], F32)
    junk8 = p.sb("junk8", [128, 8], F32)
    silu_t = p.sb("silu_t", [128, 2 * 512])
    ev = p.sb("ev", [128, 2 * 512], F32)
    hev = p.sb("hev", [128, 2 * 512], F32)
    onesrow = p.sb("onesrow", [1, 128])
    onesrow_f = p.sb("onesrow_f", [1, 128], F32)

    p_wo = [p.ps("p_woA", [128, 512]), p.ps("p_woB", [128, 512])]
    p_g = [p.ps("p_gA", [128, 512]), p.ps("p_gB", [128, 512])]
    p_u = [p.ps("p_uA", [128, 512]), p.ps("p_uB", [128, 512])]
    p_tp = [p.ps("p_tpA", [128, 128], BF16), p.ps("p_tpB", [128, 128], BF16)]

    ld = p.sem("ld")
    ldg = p.sem("ldg")
    msd = p.sem("msd")
    ldwp = [p.sem("ldwA"), p.sem("ldwB")]
    ast = [p.sem("astA"), p.sem("astB")]
    hst = [p.sem("hstA"), p.sem("hstB")]
    scl = p.sem("scl")
    wrc = p.sem("wrc")
    womm = p.sem("womm")
    woev = p.sem("woev")
    xqd = p.sem("xqd")
    wqd = p.sem("wqd")
    wtp = p.sem("wtp")
    wte = p.sem("wte")
    gmm = p.sem("gmm")
    hev_s = p.sem("hev_s")
    hxd = p.sem("hxd")
    st = p.sem("st")
    block = p.start()

    @block.sync
    def _(s):
        p.incd(s.dma_start(out=parts_t[:], in_=parts_in.rearrange("a b -> (a b)").rearrange("(o c) -> o c", o=1)), ld)
        p.incd(s.dma_start(out=clip_t[:], in_=clip_in[:, :]), ld)
        for i in range(DC):
            p.incd(s.dma_start(out=xq[:, i * TH:(i + 1) * TH], in_=xn_in[i * 128:(i + 1) * 128, :]), ld)
            p.incd(s.dma_start(out=ctx_t[:, i * TH:(i + 1) * TH], in_=ctx_in[i * 128:(i + 1) * 128, :]), ld)

    @block.gpsimd
    def _(g):
        p.incd(g.dma_start(out=id_bf[:], in_=id_in[:, :]), ldg)
        for i in range(DC):
            p.incd(g.dma_start(out=woT[:, i * 1024:(i + 1) * 1024], in_=woT_in[i * 128:(i + 1) * 128, :]), ldg)
        g.memset(onesrow[:], 1.0)
        g.memset(onesrow_f[:], 1.0)
        p.inc(g.memset(hmax_c[:], 0.0), msd)

    if P3_STAGE >= 4:
        @block.gpsimd
        def _(g):
            for wi in range(2 * IC):
                src = wg_in if wi < IC else wu_in
                row = (wi % IC) * 128
                if wi >= 2:
                    g.wait_ge(wqd, wi - 1)
                p.incd(g.dma_start(out=wtile[:, (wi % 2) * 1024:(wi % 2 + 1) * 1024],
                                   in_=src[row:row + 128, :]), ldwp[wi % 2])

    @block.vector
    def _(v):
        p.wait(v, ld)
        p.inc(v.reduce_max(srow[0:1, 0:1], parts_t[:], axis=AX), scl)

    @block.scalar
    def _(sc):
        p.wait(sc, ld)
        sc.wait_ge(scl, 1)
        p.inc(sc.activation(srow[0:1, 1:2], clip_t[:], AF.Sigmoid), scl)

    @block.vector
    def _(v):
        v.wait_ge(scl, 2)
        p.inc(v.tensor_mul(srow[0:1, 0:1], srow[0:1, 0:1], srow[0:1, 1:2]), scl)

    @block.scalar
    def _(sc):
        sc.wait_ge(scl, 3)
        p.inc(sc.activation(junkc[0:1, 0:2], srow[0:1, 0:2], AF.Copy), scl)

    @block.vector
    def _(v):
        v.wait_ge(scl, 4)
        p.inc(v.tensor_scalar_mul(srow[0:1, 0:1], srow[0:1, 0:1], 1.0 / QMAX), scl)

    @block.scalar
    def _(sc):
        sc.wait_ge(scl, 5)
        p.inc(sc.activation(junkc[0:1, 0:2], srow[0:1, 0:2], AF.Copy), scl)

    @block.vector
    def _(v):
        v.wait_ge(scl, 6)
        p.inc(v.reciprocal(srow[0:1, 1:2], srow[0:1, 0:1]), scl)

    @block.scalar
    def _(sc):
        sc.wait_ge(scl, 7)
        p.inc(sc.activation(junkc[0:1, 0:2], srow[0:1, 0:2], AF.Copy), scl)

    @block.tensor
    def _(t):
        p.wait(t, ldg)
        p.wait(t, msd)
        t.wait_ge(scl, 8)
        p.inc(t.matmul(p_wo[0][:, 0:2], onesrow_f[0:1, 0:128], srow[0:1, 0:2], start=True, stop=True), scl)

    @block.vector
    def _(v):
        v.wait_ge(scl, 9)
        p.inc(v.tensor_copy(scl2[:], p_wo[0][:, 0:2]), scl)

    if P3_STAGE >= 2:
        @block.tensor
        def _(t):
            p.wait(t, ld)
            p.wait(t, scl)
            nt = 0
            for m in range(TMH):
                for o in range(DC):
                    if nt >= 2:
                        t.wait_ge(woev, nt - 1)
                    for i in range(DC):
                        ins = t.matmul(p_wo[nt % 2][:], woT[:, i * 1024 + o * 128:i * 1024 + (o + 1) * 128],
                                       ctx_t[:, i * TH + m * 512:i * TH + (m + 1) * 512],
                                       start=(i == 0), stop=(i == DC - 1))
                    p.inc(ins, womm)
                    nt += 1

        @block.vector
        def _(v):
            nt = 0
            for m in range(TMH):
                for o in range(DC):
                    v.wait_ge(womm, nt + 1)
                    if nt >= 2:
                        v.wait_ge(ast[nt % 2], 16 * (nt // 2))
                    e = ev[:, (nt % 2) * 512:(nt % 2 + 1) * 512]
                    p.inc(v.tensor_copy(e, p_wo[nt % 2][:]), woev)
                    nt += 1

        @block.sync
        def _(s):
            nt = 0
            for m in range(TMH):
                for o in range(DC):
                    s.wait_ge(woev, nt + 1)
                    if nt >= 2:
                        s.wait_ge(ast[nt % 2], 16 * (nt // 2))
                    p.incd(s.dma_start(out=attn_st[o * 128:(o + 1) * 128, m * 512:(m + 1) * 512],
                                       in_=ev[:, (nt % 2) * 512:(nt % 2 + 1) * 512]), ast[nt % 2])
                    nt += 1

    if P3_STAGE >= 3:
        @block.vector
        def _(v):
            p.wait(v, scl)
            for i in range(DC):
                xs = xq[:, i * TH:(i + 1) * TH]
                v.tensor_scalar(tmpf[:], xs, scl2[:, 1:2], MAGIC, ALU.mult, ALU.add)
                v.tensor_scalar_min(tmpf[:], tmpf[:], MAGIC + QMAX)
                v.tensor_scalar_max(tmpf[:], tmpf[:], MAGIC - QMAX - 1.0)
                p.inc(v.tensor_scalar(xs, tmpf[:], MAGIC, scl2[:, 0:1], ALU.subtract, ALU.mult), xqd)

    if P3_STAGE >= 4:
        for wi in range(2 * IC + 1):
            @block.vector
            def _(v, wi=wi):
                if wi >= 1 and P3_SUB >= 3:
                    wj = wi - 1
                    half = wj // IC
                    oc = wj % IC
                    for i in range(DC):
                        k = 8 * wj + i
                        v.wait_ge(wtp, k + 1)
                        dst = qwT[:, (half * DC + i) * I_FF + oc * 128: (half * DC + i) * I_FF + (oc + 1) * 128]
                        p.inc(v.tensor_copy(dst, p_tp[k % 2][:]), wte)
                if wi < 2 * IC:
                    v.wait_ge(ldwp[wi % 2], 16 * (wi // 2 + 1))
                    wt = wtile[:, (wi % 2) * 1024:(wi % 2 + 1) * 1024]
                    p.inc(v.reduce_max(wcol[:, 0:1], wt, axis=AX, apply_absolute_value=True), wrc)

            if wi < 2 * IC:
                @block.scalar
                def _(sc, wi=wi):
                    sc.wait_ge(wrc, 6 * wi + 1)
                    p.inc(sc.activation(junkc[:, 0:1], wcol[:, 0:1], AF.Copy), wrc)

                @block.vector
                def _(v, wi=wi):
                    v.wait_ge(wrc, 6 * wi + 2)
                    p.inc(v.tensor_scalar(wcol[:, 1:2], wcol[:, 0:1], 1.0 / QMAX, None, ALU.mult), wrc)

                @block.scalar
                def _(sc, wi=wi):
                    sc.wait_ge(wrc, 6 * wi + 3)
                    p.inc(sc.activation(junkc[:, 0:1], wcol[:, 1:2], AF.Copy), wrc)

                @block.vector
                def _(v, wi=wi):
                    v.wait_ge(wrc, 6 * wi + 4)
                    p.inc(v.reciprocal(wcol2[:, 0:1], wcol[:, 1:2]), wrc)

                @block.scalar
                def _(sc, wi=wi):
                    sc.wait_ge(wrc, 6 * wi + 5)
                    p.inc(sc.activation(junkc[:, 0:1], wcol2[:, 0:1], AF.Copy), wrc)

                @block.vector
                def _(v, wi=wi):
                    v.wait_ge(wrc, 6 * wi + 6)
                    wt = wtile[:, (wi % 2) * 1024:(wi % 2 + 1) * 1024]
                    v.tensor_scalar(tmpf[:, 0:1024], wt, wcol2[:, 0:1], MAGIC, ALU.mult, ALU.add)
                    v.tensor_scalar_min(tmpf[:, 0:1024], tmpf[:, 0:1024], MAGIC + QMAX)
                    v.tensor_scalar_max(tmpf[:, 0:1024], tmpf[:, 0:1024], MAGIC - QMAX - 1.0)
                    if wi >= 1 and P3_SUB >= 2:
                        v.wait_ge(wtp, 8 * wi)
                    p.inc(v.tensor_scalar(wq_t[:], tmpf[:, 0:1024], MAGIC, wcol[:, 1:2], ALU.subtract, ALU.mult), wqd)

        if P3_SUB >= 2:
            @block.tensor
            def _(t):
                for wi in range(2 * IC):
                    t.wait_ge(wqd, wi + 1)
                    for i in range(DC):
                        k = 8 * wi + i
                        if k >= 2 and P3_SUB >= 3:
                            t.wait_ge(wte, k - 1)
                        p.inc(t.transpose(p_tp[k % 2][:], wq_t[:, i * 128:(i + 1) * 128], id_bf[:]), wtp)

    if P3_STAGE >= 5:
        @block.tensor
        def _(t):
            p.wait(t, xqd)
            p.wait(t, wte)
            nt = 0
            for m in range(TMH):
                for o in range(IC):
                    if nt >= 2:
                        t.wait_ge(hev_s, 2 * (nt - 1))
                    for i in range(DC):
                        t.matmul(p_g[nt % 2][:], qwT[:, i * I_FF + o * 128:i * I_FF + (o + 1) * 128],
                                 xq[:, i * TH + m * 512:i * TH + (m + 1) * 512],
                                 start=(i == 0), stop=(i == DC - 1))
                    for i in range(DC):
                        ins = t.matmul(p_u[nt % 2][:], qwT[:, (DC + i) * I_FF + o * 128:(DC + i) * I_FF + (o + 1) * 128],
                                       xq[:, i * TH + m * 512:i * TH + (m + 1) * 512],
                                       start=(i == 0), stop=(i == DC - 1))
                    p.inc(ins, gmm)
                    nt += 1

        @block.scalar
        def _(sc):
            nt = 0
            for m in range(TMH):
                for o in range(IC):
                    sc.wait_ge(gmm, nt + 1)
                    p.inc(sc.activation(silu_t[:, (nt % 2) * 512:(nt % 2 + 1) * 512],
                                        p_g[nt % 2][:], AF.Silu), hev_s)
                    nt += 1

        @block.vector
        def _(v):
            p.wait(v, msd)
            nt = 0
            for m in range(TMH):
                for o in range(IC):
                    v.wait_ge(hev_s, 2 * nt + 1)
                    if nt >= 2:
                        v.wait_ge(hst[nt % 2], 16 * (nt // 2))
                    e = hev[:, (nt % 2) * 512:(nt % 2 + 1) * 512]
                    v.tensor_mul(e, p_u[nt % 2][:], silu_t[:, (nt % 2) * 512:(nt % 2 + 1) * 512])
                    p.inc(v.reduce_max(hmax_buf[:, nt:nt + 1], e, axis=AX, apply_absolute_value=True), hev_s)
                    nt += 1

        @block.scalar
        def _(sc):
            p.wait(sc, hev_s)
            p.inc(sc.activation(junk8[:], hmax_buf[:, TMH * IC - 8:TMH * IC], AF.Copy), hxd)

        @block.vector
        def _(v):
            v.wait_ge(hxd, 1)
            p.inc(v.reduce_max(hmax_c[:, 0:1], hmax_buf[:, 0:TMH * IC],
                               axis=AX, apply_absolute_value=True), hxd)

        @block.sync
        def _(s):
            nt = 0
            for m in range(TMH):
                for o in range(IC):
                    s.wait_ge(hev_s, 2 * nt + 2)
                    if nt >= 2:
                        s.wait_ge(hst[nt % 2], 16 * (nt // 2))
                    p.incd(s.dma_start(out=h_st[o * 128:(o + 1) * 128, m * 512:(m + 1) * 512],
                                       in_=hev[:, (nt % 2) * 512:(nt % 2 + 1) * 512]), hst[nt % 2])
                    nt += 1
            p.wait(s, hxd)
            p.incd(s.dma_start(out=hmax_st[0:1, :], in_=hmax_c[:, 0:1]), st)

    if P3_DBG:
        @block.sync
        def _(s):
            p.wait(s, xqd)
            p.wait(s, wte)
            for i in range(DC):
                p.incd(s.dma_start(out=xq_dbg[i * 128:(i + 1) * 128, :], in_=xq[:, i * TH:(i + 1) * TH]), st)
                p.incd(s.dma_start(out=qwT_dbg[i * 128:(i + 1) * 128, :], in_=qwT[:, i * I_FF:(i + 1) * I_FF]), st)

    @block.sync
    def _(s):
        for sm in [ld, msd, scl, womm, woev, xqd, wqd, wtp, wte, gmm, hev_s, hxd, st] + ldwp + ast + hst:
            p.wait(s, sm)

    @block.gpsimd
    def _(g):
        p.wait(g, ldg)

    return p



def build_p4a1():
    p = Prog()
    h_in = p.din("h_st", [I_FF, TH])
    parts_in = p.din("hmax_parts", [8, 128])
    clip_in = p.din("clip_down", [1, 1])
    wd_in = p.din("w_down", [D, I_FF])
    id_in = p.din("ident_f", [128, 128])

    ffn0_st = p.dout("ffn0_st", [D, TH])

    hq = p.sb("hq", [128, IC * TH])
    hstg = p.sb("hstg", [128, 2 * TH], F32)
    qwdT = p.sb("qwdT", [128, IC * 1024])
    wtile = p.sb("wtile", [128, 2 * I_FF], F32)
    tmpf = p.sb("tmpf", [128, I_FF], F32)
    wq_t = p.sb("wq_t", [128, I_FF])
    parts_t = p.sb("parts_t", [1, 1024], F32)
    clip_t = p.sb("clip_t", [1, 1], F32)
    srow = p.sb("srow", [1, 2], F32)
    scl2 = p.sb("scl2", [128, 2], F32)
    wcol = p.sb("wcol", [128, 2], F32)
    wcol2 = p.sb("wcol2", [128, 1], F32)
    junkc = p.sb("junkc", [128, 2], F32)
    id_bf = p.sb("id_bf", [128, 128])
    onesrow_f = p.sb("onesrow_f", [1, 128], F32)
    ev = p.sb("ev", [128, 2 * 512], F32)

    p_d = [p.ps("p_dA", [128, 512]), p.ps("p_dB", [128, 512])]
    p_tp = [p.ps("p_tpA", [128, 128], BF16), p.ps("p_tpB", [128, 128], BF16)]
    p_bc = p.ps("p_bc", [128, 512])

    ld = p.sem("ld")
    ldg = p.sem("ldg")
    ldwp = [p.sem("ldwA"), p.sem("ldwB")]
    ldhp = [p.sem("ldhA"), p.sem("ldhB")]
    fst = [p.sem("fstA"), p.sem("fstB")]
    msd = p.sem("msd")
    scl = p.sem("scl")
    wrc = p.sem("wrc")
    wqd = p.sem("wqd")
    wtp = p.sem("wtp")
    wte = p.sem("wte")
    hqd = p.sem("hqd")
    dmm = p.sem("dmm")
    dev_ = p.sem("dev_")
    st = p.sem("st")
    block = p.start()

    @block.sync
    def _(s):
        p.incd(s.dma_start(out=parts_t[:], in_=parts_in.rearrange("a b -> (a b)").rearrange("(o c) -> o c", o=1)), ld)
        p.incd(s.dma_start(out=clip_t[:], in_=clip_in[:, :]), ld)

    @block.gpsimd
    def _(g):
        p.incd(g.dma_start(out=id_bf[:], in_=id_in[:, :]), ldg)
        p.inc(g.memset(onesrow_f[:], 1.0), msd)
        for wi in range(DC):
            if wi >= 2:
                g.wait_ge(wqd, wi - 1)
            p.incd(g.dma_start(out=wtile[:, (wi % 2) * I_FF:(wi % 2 + 1) * I_FF],
                               in_=wd_in[wi * 128:(wi + 1) * 128, :]), ldwp[wi % 2])

    @block.vector
    def _(v):
        p.wait(v, ld)
        p.inc(v.reduce_max(srow[0:1, 0:1], parts_t[:], axis=AX), scl)

    @block.scalar
    def _(sc):
        sc.wait_ge(scl, 1)
        p.inc(sc.activation(srow[0:1, 1:2], clip_t[:], AF.Sigmoid), scl)

    @block.vector
    def _(v):
        v.wait_ge(scl, 2)
        p.inc(v.tensor_mul(srow[0:1, 0:1], srow[0:1, 0:1], srow[0:1, 1:2]), scl)

    @block.scalar
    def _(sc):
        sc.wait_ge(scl, 3)
        p.inc(sc.activation(junkc[0:1, 0:2], srow[0:1, 0:2], AF.Copy), scl)

    @block.vector
    def _(v):
        v.wait_ge(scl, 4)
        p.inc(v.tensor_scalar_mul(srow[0:1, 0:1], srow[0:1, 0:1], 1.0 / QMAX), scl)

    @block.scalar
    def _(sc):
        sc.wait_ge(scl, 5)
        p.inc(sc.activation(junkc[0:1, 0:2], srow[0:1, 0:2], AF.Copy), scl)

    @block.vector
    def _(v):
        v.wait_ge(scl, 6)
        p.inc(v.reciprocal(srow[0:1, 1:2], srow[0:1, 0:1]), scl)

    @block.scalar
    def _(sc):
        sc.wait_ge(scl, 7)
        p.inc(sc.activation(junkc[0:1, 0:2], srow[0:1, 0:2], AF.Copy), scl)

    @block.tensor
    def _(t):
        p.wait(t, ldg)
        p.wait(t, msd)
        t.wait_ge(scl, 8)
        p.inc(t.matmul(p_bc[:, 0:2], onesrow_f[0:1, 0:128], srow[0:1, 0:2], start=True, stop=True), scl)

    @block.vector
    def _(v):
        v.wait_ge(scl, 9)
        p.inc(v.tensor_copy(scl2[:], p_bc[:, 0:2]), scl)

    for wi in range(DC + 1):
        @block.vector
        def _(v, wi=wi):
            if wi >= 1:
                wj = wi - 1
                for ic in range(IC):
                    k = IC * wj + ic
                    v.wait_ge(wtp, k + 1)
                    dst = qwdT[:, ic * 1024 + wj * 128: ic * 1024 + (wj + 1) * 128]
                    p.inc(v.tensor_copy(dst, p_tp[k % 2][:]), wte)
            if wi < DC:
                v.wait_ge(ldwp[wi % 2], 16 * (wi // 2 + 1))
                wt = wtile[:, (wi % 2) * I_FF:(wi % 2 + 1) * I_FF]
                p.inc(v.reduce_max(wcol[:, 0:1], wt, axis=AX, apply_absolute_value=True), wrc)

        if wi < DC:
            @block.scalar
            def _(sc, wi=wi):
                sc.wait_ge(wrc, 6 * wi + 1)
                p.inc(sc.activation(junkc[:, 0:1], wcol[:, 0:1], AF.Copy), wrc)

            @block.vector
            def _(v, wi=wi):
                v.wait_ge(wrc, 6 * wi + 2)
                p.inc(v.tensor_scalar(wcol[:, 1:2], wcol[:, 0:1], 1.0 / QMAX, None, ALU.mult), wrc)

            @block.scalar
            def _(sc, wi=wi):
                sc.wait_ge(wrc, 6 * wi + 3)
                p.inc(sc.activation(junkc[:, 0:1], wcol[:, 1:2], AF.Copy), wrc)

            @block.vector
            def _(v, wi=wi):
                v.wait_ge(wrc, 6 * wi + 4)
                p.inc(v.reciprocal(wcol2[:, 0:1], wcol[:, 1:2]), wrc)

            @block.scalar
            def _(sc, wi=wi):
                sc.wait_ge(wrc, 6 * wi + 5)
                p.inc(sc.activation(junkc[:, 0:1], wcol2[:, 0:1], AF.Copy), wrc)

            @block.vector
            def _(v, wi=wi):
                v.wait_ge(wrc, 6 * wi + 6)
                wt = wtile[:, (wi % 2) * I_FF:(wi % 2 + 1) * I_FF]
                v.tensor_scalar(tmpf[:], wt, wcol2[:, 0:1], MAGIC, ALU.mult, ALU.add)
                v.tensor_scalar_min(tmpf[:], tmpf[:], MAGIC + QMAX)
                v.tensor_scalar_max(tmpf[:], tmpf[:], MAGIC - QMAX - 1.0)
                if wi >= 1:
                    v.wait_ge(wtp, IC * wi)
                p.inc(v.tensor_scalar(wq_t[:], tmpf[:], MAGIC, wcol[:, 1:2], ALU.subtract, ALU.mult), wqd)

    @block.tensor
    def _(t):
        for wi in range(DC):
            t.wait_ge(wqd, wi + 1)
            for ic in range(IC):
                k = IC * wi + ic
                if k >= 2:
                    t.wait_ge(wte, k - 1)
                p.inc(t.transpose(p_tp[k % 2][:], wq_t[:, ic * 128:(ic + 1) * 128], id_bf[:]), wtp)

    @block.sync
    def _(s):
        for ic in range(IC):
            if ic >= 2:
                s.wait_ge(hqd, ic - 1)
            p.incd(s.dma_start(out=hstg[:, (ic % 2) * TH:(ic % 2 + 1) * TH],
                               in_=h_in[ic * 128:(ic + 1) * 128, :]), ldhp[ic % 2])

    @block.vector
    def _(v):
        p.wait(v, scl)
        for ic in range(IC):
            v.wait_ge(ldhp[ic % 2], 16 * (ic // 2 + 1))
            hs = hstg[:, (ic % 2) * TH:(ic % 2 + 1) * TH]
            v.tensor_scalar(tmpf[:, 0:TH], hs, scl2[:, 1:2], MAGIC, ALU.mult, ALU.add)
            v.tensor_scalar_min(tmpf[:, 0:TH], tmpf[:, 0:TH], MAGIC + QMAX)
            v.tensor_scalar_max(tmpf[:, 0:TH], tmpf[:, 0:TH], MAGIC - QMAX - 1.0)
            p.inc(v.tensor_scalar(hq[:, ic * TH:(ic + 1) * TH], tmpf[:, 0:TH], MAGIC, scl2[:, 0:1], ALU.subtract, ALU.mult), hqd)


    @block.tensor
    def _(t):
        p.wait(t, wte)
        p.wait(t, hqd)
        nt = 0
        for m in range(TMH):
            for o in range(DC):
                if nt >= 2:
                    t.wait_ge(dev_, nt - 1)
                for ic in range(IC):
                    ins = t.matmul(p_d[nt % 2][:], qwdT[:, ic * 1024 + o * 128: ic * 1024 + (o + 1) * 128],
                                   hq[:, ic * TH + m * 512:ic * TH + (m + 1) * 512],
                                   start=(ic == 0), stop=(ic == IC - 1))
                p.inc(ins, dmm)
                nt += 1

    @block.vector
    def _(v):
        nt = 0
        for m in range(TMH):
            for o in range(DC):
                v.wait_ge(dmm, nt + 1)
                if nt >= 2:
                    v.wait_ge(fst[nt % 2], 16 * (nt // 2))
                p.inc(v.tensor_copy(ev[:, (nt % 2) * 512:(nt % 2 + 1) * 512], p_d[nt % 2][:]), dev_)
                nt += 1

    @block.sync
    def _(s):
        nt = 0
        for m in range(TMH):
            for o in range(DC):
                s.wait_ge(dev_, nt + 1)
                if nt >= 2:
                    s.wait_ge(fst[nt % 2], 16 * (nt // 2))
                p.incd(s.dma_start(out=ffn0_st[o * 128:(o + 1) * 128, m * 512:(m + 1) * 512],
                                   in_=ev[:, (nt % 2) * 512:(nt % 2 + 1) * 512]), fst[nt % 2])
                nt += 1
        for sm in [ld, msd, scl, wrc, wqd, wtp, wte, hqd, dmm, dev_, st] + ldwp + ldhp + fst:
            p.wait(s, sm)

    @block.gpsimd
    def _(g):
        p.wait(g, ldg)

    return p



def build_p4a2():
    p = Prog()
    x_in = p.din("x_own", [D, TH])
    attn_in = p.din("attn_st", [D, TH])
    ffn_in = p.din("ffn0_st", [D, TH])
    eps_in = p.din("eps11", [1, 1])

    x1_st = p.dout("x1_st", [D, TH])
    xn1_st = p.dout("xn1_st", [D, TH], BF16)

    xt = p.sb("xt", [128, 2 * 512], F32)
    at = p.sb("at", [128, 2 * 512], F32)
    ft = p.sb("ft", [128, 2 * 512], F32)
    ev = p.sb("ev", [128, 2 * 512], F32)
    sqt = p.sb("sqt", [128, 2 * 512], F32)
    onescol = p.sb("onescol", [128, 1], F32)
    onesrow = p.sb("onesrow", [1, 128])
    r_mean = p.sb("r_mean", [1, TH], F32)
    r_var = p.sb("r_var", [1, TH], F32)
    r_rstd = p.sb("r_rstd", [1, 512], F32)
    r_tmp = p.sb("r_tmp", [1, 512], F32)
    r_tmp2 = p.sb("r_tmp2", [1, 512], F32)
    rb_sx = p.sb("rb_sx", [1, 512])
    rb_ox = p.sb("rb_ox", [1, 512])
    eps_t = p.sb("eps_t", [1, 1], F32)
    bct = p.sb("bct", [128, 2 * 1024])
    x1t = p.sb("x1t", [128, 2 * 512], F32)
    xnev = p.sb("xnev", [128, 2 * 512])

    p_sum = [p.ps("p_sum0", [1, 512]), p.ps("p_sum1", [1, 512])]
    p_sq = [p.ps("p_sq0", [1, 512]), p.ps("p_sq1", [1, 512])]
    p_bc = p.ps("p_bc", [128, 512])

    ld = p.sem("ld")
    msd = p.sem("msd")
    evd = p.sem("evd")
    sqd = p.sem("sqd")
    mmd = p.sem("mmd")
    sev = p.sem("sev")
    rows = p.sem("rows")
    rowsb = p.sem("rowsb")
    bcmm = p.sem("bcmm")
    bcev = p.sem("bcev")
    xldp = [p.sem("xldA"), p.sem("xldB")]
    x2p = [p.sem("x2A"), p.sem("x2B")]
    xne = p.sem("xne")
    stp = [p.sem("stA"), p.sem("stB")]
    st2p = [p.sem("s2A"), p.sem("s2B")]
    block = p.start()

    @block.gpsimd
    def _(g):
        g.memset(onescol[:], 1.0)
        p.inc(g.memset(onesrow[:], 1.0), msd)

    @block.sync
    def _(s):
        p.incd(s.dma_start(out=eps_t[:], in_=eps_in[:, :]), ld)

    def emit_loads(g, m):
        for o in range(DC):
            nt = m * DC + o
            if nt >= 2:
                g.wait_ge(evd, nt - 1)
            sl = (slice(o * 128, (o + 1) * 128), slice(m * 512, (m + 1) * 512))
            p.incd(g.dma_start(out=xt[:, (nt % 2) * 512:(nt % 2 + 1) * 512], in_=x_in[sl[0], sl[1]]), xldp[nt % 2])
            p.incd(g.dma_start(out=at[:, (nt % 2) * 512:(nt % 2 + 1) * 512], in_=attn_in[sl[0], sl[1]]), xldp[nt % 2])
            p.incd(g.dma_start(out=ft[:, (nt % 2) * 512:(nt % 2 + 1) * 512], in_=ffn_in[sl[0], sl[1]]), xldp[nt % 2])

    @block.gpsimd
    def _(g):
        for m in range(TMH):
            emit_loads(g, m)

    @block.vector
    def _(v):
        for m in range(TMH):
            for o in range(DC):
                nt = m * DC + o
                v.wait_ge(xldp[nt % 2], 48 * (nt // 2 + 1))
                if nt >= 2:
                    v.wait_ge(stp[nt % 2], 16 * (nt // 2))
                    v.wait_ge(mmd, nt - 1)
                e = ev[:, (nt % 2) * 512:(nt % 2 + 1) * 512]
                v.tensor_add(e, xt[:, (nt % 2) * 512:(nt % 2 + 1) * 512], at[:, (nt % 2) * 512:(nt % 2 + 1) * 512])
                p.inc(v.tensor_add(e, e, ft[:, (nt % 2) * 512:(nt % 2 + 1) * 512]), evd)
            v.wait_ge(mmd, DC * (m + 1))
            mm = r_mean[0:1, m * 512:(m + 1) * 512]
            vv = r_var[0:1, m * 512:(m + 1) * 512]
            v.tensor_scalar_mul(mm, p_sum[m % 2][:], 1.0 / D)
            v.tensor_scalar_mul(vv, p_sq[m % 2][:], 1.0 / D)
            v.tensor_mul(r_tmp2[:], mm, mm)
            p.inc(v.tensor_sub(vv, vv, r_tmp2[:]), sev)

    @block.scalar
    def _(sc):
        for m in range(TMH):
            for o in range(DC):
                nt = m * DC + o
                sc.wait_ge(evd, nt + 1)
                if nt >= 2:
                    sc.wait_ge(mmd, nt - 1)
                p.inc(sc.activation(sqt[:, (nt % 2) * 512:(nt % 2 + 1) * 512],
                                    ev[:, (nt % 2) * 512:(nt % 2 + 1) * 512], AF.Square), sqd)

    @block.tensor
    def _(t):
        p.wait(t, msd)
        for m in range(TMH):
            if m > 1:
                t.wait_ge(sev, m - 1)
            for o in range(DC):
                nt = m * DC + o
                t.wait_ge(evd, nt + 1)
                t.wait_ge(sqd, nt + 1)
                ins0 = t.matmul(p_sum[m % 2][:], onescol[:], ev[:, (nt % 2) * 512:(nt % 2 + 1) * 512],
                                start=(o == 0), stop=(o == DC - 1))
                ins = t.matmul(p_sq[m % 2][:], onescol[:], sqt[:, (nt % 2) * 512:(nt % 2 + 1) * 512],
                               start=(o == 0), stop=(o == DC - 1))
                p.inc(ins, mmd)

    @block.sync
    def _(s):
        for m in range(TMH):
            for o in range(DC):
                nt = m * DC + o
                s.wait_ge(evd, nt + 1)
                if nt >= 2:
                    s.wait_ge(stp[nt % 2], 16 * (nt // 2))
                p.incd(s.dma_start(out=x1_st[o * 128:(o + 1) * 128, m * 512:(m + 1) * 512],
                                   in_=ev[:, (nt % 2) * 512:(nt % 2 + 1) * 512]), stp[nt % 2])

    for m in range(TMH):
        @block.scalar
        def _(sc, m=m):
            sc.wait_ge(sev, m + 1)
            if m == 0:
                p.wait(sc, ld)
            sc.activation(r_tmp[:], r_var[0:1, m * 512:(m + 1) * 512], AF.Ln, bias=eps_t[:])
            p.inc(sc.activation(r_rstd[:], r_tmp[:], AF.Exp, scale=-0.5), rows)

        @block.vector
        def _(v, m=m):
            v.wait_ge(rows, 2 * m + 1)
            v.tensor_mul(r_tmp[:], r_mean[0:1, m * 512:(m + 1) * 512], r_rstd[:])
            p.inc(v.tensor_scalar_mul(r_tmp[:], r_tmp[:], -1.0), rows)

        @block.scalar
        def _(sc, m=m):
            sc.wait_ge(rows, 2 * m + 2)
            sc.activation(rb_sx[:], r_rstd[:], AF.Copy)
            p.inc(sc.activation(rb_ox[:], r_tmp[:], AF.Copy), rowsb)

        @block.tensor
        def _(t, m=m):
            t.wait_ge(rowsb, m + 1)
            if m >= 1:
                t.wait_ge(bcev, 2 * m - 1)
            p.inc(t.matmul(p_bc[:], onesrow[0:1, 0:128], rb_sx[0:1, :], start=True, stop=True), bcmm)
            t.wait_ge(bcev, 2 * m + 1)
            p.inc(t.matmul(p_bc[:], onesrow[0:1, 0:128], rb_ox[0:1, :], start=True, stop=True), bcmm)

        @block.vector
        def _(v, m=m):
            q = m % 2
            v.wait_ge(bcmm, 2 * m + 1)
            p.inc(v.tensor_copy(bct[:, q * 1024:q * 1024 + 512], p_bc[:]), bcev)
            v.wait_ge(bcmm, 2 * m + 2)
            p.inc(v.tensor_copy(bct[:, q * 1024 + 512:q * 1024 + 1024], p_bc[:]), bcev)

        @block.gpsimd
        def _(g, m=m):
            for o in range(DC):
                k = m * DC + o
                if k >= 2:
                    g.wait_ge(xne, k - 1)
                g.wait_ge(stp[k % 2], 16 * (k // 2 + 1))
                p.incd(g.dma_start(out=x1t[:, (k % 2) * 512:(k % 2 + 1) * 512],
                                   in_=x1_st[o * 128:(o + 1) * 128, m * 512:(m + 1) * 512]), x2p[k % 2])

        @block.vector
        def _(v, m=m):
            q = m % 2
            p.wait(v, bcev)
            for o in range(DC):
                k = m * DC + o
                v.wait_ge(x2p[k % 2], 16 * (k // 2 + 1))
                if k >= 2:
                    v.wait_ge(st2p[k % 2], 16 * (k // 2))
                e = xnev[:, (k % 2) * 512:(k % 2 + 1) * 512]
                v.tensor_mul(e, x1t[:, (k % 2) * 512:(k % 2 + 1) * 512], bct[:, q * 1024:q * 1024 + 512])
                p.inc(v.tensor_add(e, e, bct[:, q * 1024 + 512:q * 1024 + 1024]), xne)

        @block.sync
        def _(s, m=m):
            for o in range(DC):
                k = m * DC + o
                s.wait_ge(xne, k + 1)
                if k >= 2:
                    s.wait_ge(st2p[k % 2], 16 * (k // 2))
                p.incd(s.dma_start(out=xn1_st[o * 128:(o + 1) * 128, m * 512:(m + 1) * 512],
                                   in_=xnev[:, (k % 2) * 512:(k % 2 + 1) * 512]), st2p[k % 2])

    @block.sync
    def _(s):
        for sm in [ld, msd, evd, sqd, mmd, sev, rows, rowsb, bcmm, bcev, xne] + xldp + x2p + stp + st2p:
            p.wait(s, sm)

    return p



def build_p4b():
    p = Prog()
    xn1_in = p.din("xn1_own", [D, TH], BF16)
    win_in = p.din("W_inT", [D, 4480])
    z_st = p.dout("z_st", [D_INNER, TH], BF16)
    xbc_st = p.dout("xbc_st", [2304, TH], BF16)
    dt_st = p.dout("dt_st", [32, TH])

    xn1 = p.sb("xn1", [128, DC * TH])
    winT = p.sb("winT", [128, DC * 4480])
    ev = p.sb("ev", [128, 2 * 512])
    evf = p.sb("evf", [128, 2 * 512], F32)

    p_o = [p.ps("p_oA", [128, 512]), p.ps("p_oB", [128, 512])]

    ld = p.sem("ld")
    ldg = p.sem("ldg")
    omm = p.sem("omm")
    oev = p.sem("oev")
    st = p.sem("st")
    ost = [p.sem("ostA"), p.sem("ostB")]
    block = p.start()

    @block.sync
    def _(s):
        for i in range(DC):
            p.incd(s.dma_start(out=xn1[:, i * TH:(i + 1) * TH], in_=xn1_in[i * 128:(i + 1) * 128, :]), ld)

    @block.gpsimd
    def _(g):
        for i in range(DC):
            p.incd(g.dma_start(out=winT[:, i * 4480:(i + 1) * 4480], in_=win_in[i * 128:(i + 1) * 128, :]), ldg)

    NO = 35

    @block.tensor
    def _(t):
        p.wait(t, ld)
        p.wait(t, ldg)
        nt = 0
        for m in range(TMH):
            for o in range(NO):
                if nt >= 2:
                    t.wait_ge(oev, nt - 1)
                for i in range(DC):
                    ins = t.matmul(p_o[nt % 2][:], winT[:, i * 4480 + o * 128:i * 4480 + (o + 1) * 128],
                                   xn1[:, i * TH + m * 512:i * TH + (m + 1) * 512],
                                   start=(i == 0), stop=(i == DC - 1))
                p.inc(ins, omm)
                nt += 1

    @block.vector
    def _(v):
        nt = 0
        for m in range(TMH):
            for o in range(NO):
                v.wait_ge(omm, nt + 1)
                if nt >= 2:
                    v.wait_ge(ost[nt % 2], 16 * (nt // 2))
                if o == 34:
                    p.inc(v.tensor_copy(evf[:, (nt % 2) * 512:(nt % 2 + 1) * 512], p_o[nt % 2][:]), oev)
                else:
                    p.inc(v.tensor_copy(ev[:, (nt % 2) * 512:(nt % 2 + 1) * 512], p_o[nt % 2][:]), oev)
                nt += 1

    @block.sync
    def _(s):
        nt = 0
        for m in range(TMH):
            for o in range(NO):
                s.wait_ge(oev, nt + 1)
                if nt >= 2:
                    s.wait_ge(ost[nt % 2], 16 * (nt // 2))
                if o < 16:
                    dst = z_st[o * 128:(o + 1) * 128, m * 512:(m + 1) * 512]
                    p.incd(s.dma_start(out=dst, in_=ev[:, (nt % 2) * 512:(nt % 2 + 1) * 512]), ost[nt % 2])
                elif o < 34:
                    dst = xbc_st[(o - 16) * 128:(o - 15) * 128, m * 512:(m + 1) * 512]
                    p.incd(s.dma_start(out=dst, in_=ev[:, (nt % 2) * 512:(nt % 2 + 1) * 512]), ost[nt % 2])
                else:
                    dst = dt_st[0:32, m * 512:(m + 1) * 512]
                    p.incd(s.dma_start(out=dst, in_=evf[0:32, (nt % 2) * 512:(nt % 2 + 1) * 512]), ost[nt % 2])
                nt += 1
        p.wait(s, st)
        p.wait(s, ost[0])
        p.wait(s, ost[1])

    return p




P5_DBG = False
P5_KD = 0
P5_HD = 16

NCH = 18
NTC = 16
CL = 128
XBW = CL * NTC + 3


def build_p5():
    p = Prog()
    xbc_in = p.din("xbc_own", [2304, TH], BF16)
    tail_in = p.din("xbc_tail", [2304, 3], BF16)
    dtr_in = p.din("dtr_in", [NH, TH])
    s0_in = p.din("s0_in", [128, D_INNER])
    cw_in = p.din("conv_w", [128, NCH * DCONV])
    cb_in = p.din("conv_b", [128, NCH])
    dtb_in = p.din("dt_bias", [NH, 1])
    alog_in = p.din("a_log", [NH, 1])
    dcol_in = p.din("d_col", [128, NTC])
    esel_in = p.din("esel", [NH, NH * CL])
    triu_in = p.din("triu_m", [128, 128], mybir.dt.uint8)
    id_in = p.din("ident_f", [128, 128])
    idb_in = p.din("id_bf", [128, 128], BF16)

    y_st = p.dout("y_st", [D_INNER, TH])
    sfin_st = p.dout("sfin_st", [128, D_INNER])
    if P5_DBG:
        dbg_names = ["cb", "sel", "gexp", "eci", "mT", "ctil", "xdt", "sbf", "yps", "cumtm", "ncum", "dttm", "dtw", "hT", "xstm"]
        dbg_out = {n: p.dout("dbg_" + n, [128, 128]) for n in dbg_names}

    xbc = p.sb("xbc", [128, NCH * XBW])
    dtr = p.sb("dtr", [NH, TH], F32)
    tsp1 = p.sb("tsp1", [NH, TH], F32)
    tsp2 = p.sb("tsp2", [NH, TH], F32)
    dt_sb = p.sb("dt_sb", [NH, TH], F32)
    cum = p.sb("cum", [NH, TH], F32)
    dta = p.sb("dta", [NH, TH], F32)
    S_nm = p.sb("S_nm", [128, D_INNER], F32)
    xbcc = p.sb("xbcc", [128, NCH * CL], F32)
    cacc = p.sb("cacc", [128, 2 * CL], F32)
    xsb = p.sb("xsb", [128, 16 * CL])
    xs_tm = p.sb("xs_tm", [128, 16 * CL])
    btm_sb = p.sb("btm_sb", [128, CL])
    hT_sb = p.sb("hT_sb", [128, CL], F32)
    B_bf = p.sb("B_bf", [128, CL])
    C_bf = p.sb("C_bf", [128, CL])
    w_fm = p.sb("w_fm", [NH, CL], F32)
    cum_tm = p.sb("cum_tm", [128, NH], F32)
    ncum_tm = p.sb("ncum_tm", [128, NH], F32)
    dt_tm = p.sb("dt_tm", [128, NH], F32)
    w_tm = p.sb("w_tm", [128, NH], F32)
    dtw_tm = p.sb("dtw_tm", [128, NH], F32)
    clrow = p.sb("clrow", [1, NH], F32)
    pc_row = p.sb("pc_row", [1, NH], F32)
    pc_bc = p.sb("pc_bc", [128, NH], F32)
    sel_sb = p.sb("sel_sb", [128, 2 * CL], F32)
    gexp = p.sb("gexp", [128, 2 * CL], F32)
    mT = p.sb("mT", [128, 2 * CL])
    ctil = p.sb("ctil", [128, 2 * CL])
    eci = p.sb("eci", [128, 2 * CL])
    xdtb = p.sb("xdtb", [128, 2 * 64])
    xhat = p.sb("xhat", [128, 2 * CL])
    sbf = p.sb("sbf", [128, 2 * 64])
    y_buf = p.sb("y_buf", [128, 2 * CL], F32)
    dbg_sb = p.sb("dbg_sb", [128, 16 * 128], F32) if P5_DBG else None
    cw_sb = p.sb("cw_sb", [128, NCH * DCONV], F32)
    cb_sb = p.sb("cb_sb", [128, NCH], F32)
    dtb_sb = p.sb("dtb_sb", [NH, 1], F32)
    alog_sb = p.sb("alog_sb", [NH, 1], F32)
    acol = p.sb("acol", [NH, 1], F32)
    onecol = p.sb("onecol", [NH, 1], F32)
    dcol_sb = p.sb("dcol_sb", [128, NTC], F32)
    esel_sb = p.sb("esel_sb", [NH, NH * CL], F32)
    triu_sb = p.sb("triu_sb", [128, 128], mybir.dt.uint8)
    negt = p.sb("negt", [128, 128], F32)
    id_f = p.sb("id_f", [128, 128], F32)
    idb_sb = p.sb("idb_sb", [128, 128], BF16)
    ones_f = p.sb("ones_f", [1, 128], F32)

    p_tr = p.ps("p_tr", [128, 512])
    p_ht = p.ps("p_ht", [128, 512])
    p_xs = [p.ps("p_xs0", [128, 1024], BF16), p.ps("p_xs1", [128, 1024], BF16)]
    p_cb = [p.ps("p_cb0", [128, 512]), p.ps("p_cb1", [128, 512])]
    p_Y = p.ps("p_Y", [128, 512])
    p_S = p.ps("p_S", [128, 512])

    ld = p.sem("ld")
    ldg = p.sem("ldg")
    msd = p.sem("msd")
    pre = p.sem("pre")
    cnv = p.sem("cnv")
    sil = p.sem("sil")
    bcst = p.sem("bcst")
    sca = p.sem("sca")
    wfm_s = p.sem("wfm_s")
    trm = p.sem("trm")
    evt = p.sem("evt")
    pcx = p.sem("pcx")
    nct = p.sem("nct")
    htm = p.sem("htm")
    hte = p.sem("hte")
    xst = p.sem("xst")
    xse = p.sem("xse")
    hbc = p.sem("hbc")
    selc = p.sem("selc")
    gxc = p.sem("gxc")
    ecic = p.sem("ecic")
    mtc = p.sem("mtc")
    ctc = p.sem("ctc")
    xdtc = p.sem("xdtc")
    xhtc = p.sem("xhtc")
    sbfc = p.sem("sbfc")
    ymc = p.sem("ymc")
    sdm = p.sem("sdm")
    sup = p.sem("sup")
    yev = p.sem("yev")
    ystp = [p.sem("ystA"), p.sem("ystB")]
    sfd = p.sem("sfd")
    dbgc = p.sem("dbgc") if P5_DBG else None
    _dbg_slot = [0]
    _dbg_map = {}

    def dbg(v, name, ap):
        if not P5_DBG:
            return
        i = _dbg_slot[0]
        _dbg_slot[0] += 1
        _dbg_map[name] = i
        pp, ff = ap.shape
        p.inc(v.tensor_copy(dbg_sb[0:pp, i * 128:i * 128 + ff], ap), dbgc)
    block = p.start()

    @block.sync
    def _(s):
        for o in range(NCH):
            p.incd(s.dma_start(out=xbc[:, o * XBW:o * XBW + 3], in_=tail_in[o * 128:(o + 1) * 128, :]), ld)
            p.incd(s.dma_start(out=xbc[:, o * XBW + 3:(o + 1) * XBW], in_=xbc_in[o * 128:(o + 1) * 128, :]), ld)
        p.incd(s.dma_start(out=dtr[:], in_=dtr_in[:, :]), ld)
        p.incd(s.dma_start(out=S_nm[:], in_=s0_in[:, :]), ld)
        p.incd(s.dma_start(out=cw_sb[:], in_=cw_in[:, :]), ld)
        p.incd(s.dma_start(out=cb_sb[:], in_=cb_in[:, :]), ld)
        p.incd(s.dma_start(out=dtb_sb[:], in_=dtb_in[:, :]), ld)
        p.incd(s.dma_start(out=alog_sb[:], in_=alog_in[:, :]), ld)
        p.incd(s.dma_start(out=dcol_sb[:], in_=dcol_in[:, :]), ld)
        p.incd(s.dma_start(out=esel_sb[:], in_=esel_in[:, :]), ld)
        p.incd(s.dma_start(out=triu_sb[:], in_=triu_in[:, :]), ld)
        p.incd(s.dma_start(out=id_f[:], in_=id_in[:, :]), ld)
        p.incd(s.dma_start(out=idb_sb[:], in_=idb_in[:, :]), ld)

    @block.gpsimd
    def _(g):
        g.memset(negt[:], -1e30)
        g.memset(ones_f[:], 1.0)
        p.inc(g.memset(onecol[:], 1.0), msd)

    @block.vector
    def _(v):
        p.wait(v, ld)
        p.inc(v.tensor_scalar(tsp1[:], dtr[:], dtb_sb[:], None, ALU.add), pre)

    @block.scalar
    def _(sc):
        p.wait(sc, msd)
        sc.wait_ge(pre, 1)
        sc.activation(tsp2[:], tsp1[:], AF.Abs)
        sc.activation(tsp2[:], tsp2[:], AF.Exp, scale=-1.0)
        sc.activation(tsp2[:], tsp2[:], AF.Ln, bias=onecol[:])
        sc.activation(dt_sb[:], tsp1[:], AF.Relu)
        p.inc(sc.activation(acol[:], alog_sb[:], AF.Exp), pre)

    @block.vector
    def _(v):
        v.wait_ge(pre, 2)
        v.tensor_add(dt_sb[:], dt_sb[:], tsp2[:])
        p.inc(v.tensor_scalar(dta[:], dt_sb[:], acol[:], -1.0, ALU.mult, ALU.mult), pre)

    for k in range(NTC):
        @block.vector
        def _(v, k=k):
            if k == 0:
                p.wait(v, ld)
            for o in range(NCH):
                kk = k * NCH + o
                if kk >= 2:
                    v.wait_ge(sil, kk - 1)
                a = cacc[:, (kk % 2) * CL:(kk % 2 + 1) * CL]
                src = lambda tap: xbc[:, o * XBW + k * CL + tap: o * XBW + k * CL + tap + CL]
                v.tensor_scalar(a, src(0), cw_sb[:, o * DCONV:o * DCONV + 1], None, ALU.mult)
                for tap in range(1, DCONV):
                    ins = v.scalar_tensor_tensor(a, src(tap), cw_sb[:, o * DCONV + tap:o * DCONV + tap + 1],
                                                 a, ALU.mult, ALU.add)
                p.inc(ins, cnv)

        @block.scalar
        def _(sc, k=k):
            for o in range(NCH):
                kk = k * NCH + o
                sc.wait_ge(cnv, kk + 1)
                if k >= 1:
                    sc.wait_ge(yev, 16 * (k - 1) + min(o, 15) + 1)
                p.inc(sc.activation(xbcc[:, o * CL:(o + 1) * CL], cacc[:, (kk % 2) * CL:(kk % 2 + 1) * CL],
                                    AF.Silu, bias=cb_sb[:, o:o + 1]), sil)

        @block.vector
        def _(v, k=k):
            for o in range(16):
                v.wait_ge(sil, k * NCH + o + 1)
                p.inc(v.tensor_copy(xsb[:, o * CL:(o + 1) * CL], xbcc[:, o * CL:(o + 1) * CL]), bcst)
            v.wait_ge(sil, k * NCH + 17)
            p.inc(v.tensor_copy(B_bf[:], xbcc[:, 16 * CL:17 * CL]), bcst)
            v.wait_ge(sil, k * NCH + 18)
            p.inc(v.tensor_copy(C_bf[:], xbcc[:, 17 * CL:18 * CL]), bcst)
            p.inc(v.tensor_tensor_scan(cum[:, k * CL:(k + 1) * CL], dta[:, k * CL:(k + 1) * CL],
                                       dta[:, k * CL:(k + 1) * CL], 0.0, ALU.add, ALU.bypass), sca)

        @block.scalar
        def _(sc, k=k):
            sc.wait_ge(sca, k + 1)
            if k >= 1:
                sc.wait_ge(trm, 5 * (k - 1) + 3)
            p.inc(sc.activation(w_fm[:], cum[:, k * CL:(k + 1) * CL], AF.Exp, scale=-1.0,
                                bias=cum[:, k * CL + CL - 1:k * CL + CL]), wfm_s)

        @block.tensor
        def _(t, k=k):
            if k == 0:
                p.wait(t, ld)
            t.wait_ge(sca, k + 1)
            t.wait_ge(evt, 5 * k)
            p.inc(t.transpose(p_tr[:, 0:NH], cum[:, k * CL:(k + 1) * CL], id_f[0:NH, 0:NH]), trm)
            t.wait_ge(evt, 5 * k + 1)
            p.inc(t.transpose(p_tr[:, 0:NH], dt_sb[:, k * CL:(k + 1) * CL], id_f[0:NH, 0:NH]), trm)
            t.wait_ge(wfm_s, k + 1)
            t.wait_ge(evt, 5 * k + 2)
            p.inc(t.transpose(p_tr[:, 0:NH], w_fm[:], id_f[0:NH, 0:NH]), trm)
            t.wait_ge(evt, 5 * k + 3)
            p.inc(t.transpose(p_tr[0:1, 0:NH], cum[:, k * CL + CL - 1:k * CL + CL], id_f[0:NH, 0:NH]), trm)
            t.wait_ge(pcx, k + 1)
            t.wait_ge(evt, 5 * k + 4)
            p.inc(t.matmul(p_tr[:, 0:NH], ones_f[0:1, :], pc_row[0:1, :], start=True, stop=True), trm)

        @block.vector
        def _(v, k=k):
            v.wait_ge(trm, 5 * k + 1)
            if k >= 1:
                v.wait_ge(gxc, 32 * k)
            p.inc(v.tensor_copy(cum_tm[:], p_tr[:, 0:NH]), evt)
            v.tensor_scalar_mul(ncum_tm[:], p_tr[:, 0:NH], -1.0)
            v.wait_ge(trm, 5 * k + 2)
            p.inc(v.tensor_copy(dt_tm[:], p_tr[:, 0:NH]), evt)
            v.wait_ge(trm, 5 * k + 3)
            p.inc(v.tensor_copy(w_tm[:], p_tr[:, 0:NH]), evt)
            v.wait_ge(trm, 5 * k + 4)
            p.inc(v.tensor_copy(clrow[:], p_tr[0:1, 0:NH]), evt)
            v.wait_ge(trm, 5 * k + 5)
            if k >= 1:
                v.wait_ge(sup, 32 * k)
            p.inc(v.tensor_copy(pc_bc[:], p_tr[:, 0:NH]), evt)

        @block.scalar
        def _(sc, k=k):
            sc.wait_ge(evt, 5 * k + 4)
            p.inc(sc.activation(pc_row[:], clrow[:], AF.Exp), pcx)

        @block.gpsimd
        def _(g, k=k):
            g.wait_ge(evt, 5 * k + 3)
            if k >= 1:
                g.wait_ge(xhtc, 32 * k)
            p.inc(g.tensor_mul(dtw_tm[:], dt_tm[:], w_tm[:]), nct)

        @block.tensor
        def _(t, k=k):
            t.wait_ge(bcst, 18 * k + 18)
            t.wait_ge(hte, k)
            p.inc(t.matmul(p_ht[:, 0:CL], B_bf[:], C_bf[:], start=True, stop=True), htm)
            for i in range(17):
                gi = 17 * k + i
                if gi >= 2:
                    t.wait_ge(xse, gi - 1)
                if i == 0:
                    ins = t.transpose(p_xs[gi % 2][:, 0:CL], B_bf[:], idb_sb[:])
                else:
                    ins = t.transpose(p_xs[gi % 2][:, 0:CL], xsb[:, (i - 1) * CL:i * CL], idb_sb[:])
                p.inc(ins, xst)

        @block.vector
        def _(v, k=k):
            v.wait_ge(htm, k + 1)
            if k >= 1:
                v.wait_ge(mtc, 32 * k)
            p.inc(v.tensor_copy(hT_sb[:], p_ht[:, 0:CL]), hte)
            for i in range(17):
                gi = 17 * k + i
                v.wait_ge(xst, gi + 1)
                if i == 0:
                    if k >= 1:
                        v.wait_ge(sdm, 16 * k)
                    p.inc(v.tensor_copy(btm_sb[:], p_xs[gi % 2][:, 0:CL]), xse)
                else:
                    p.inc(v.tensor_copy(xs_tm[:, (i - 1) * CL:i * CL], p_xs[gi % 2][:, 0:CL]), xse)

        @block.tensor
        def _(t, k=k):
            for h in range(NH):
                o, half = divmod(h, 2)
                H = 32 * k + h
                if H >= 2:
                    t.wait_ge(selc, H - 1)
                    t.wait_ge(ecic, H - 1)
                p.inc(t.matmul(p_cb[h % 2][:, 0:CL], esel_sb[:, h * CL:(h + 1) * CL],
                               cum[:, k * CL:(k + 1) * CL], start=True, stop=True), hbc)
                t.wait_ge(yev, 16 * k + o)
                t.wait_ge(mtc, H + 1)
                t.wait_ge(xdtc, H + 1)
                p.inc(t.matmul(p_Y[half * 64:(half + 1) * 64, 0:CL], xdtb[:, (h % 2) * 64:(h % 2 + 1) * 64],
                               mT[:, (h % 2) * CL:(h % 2 + 1) * CL],
                               start=True, stop=False, tile_position=(0, 64 * half)), ymc)
                t.wait_ge(ctc, H + 1)
                t.wait_ge(sbfc, H + 1)
                p.inc(t.matmul(p_Y[half * 64:(half + 1) * 64, 0:CL], sbf[:, (h % 2) * 64:(h % 2 + 1) * 64],
                               ctil[:, (h % 2) * CL:(h % 2 + 1) * CL],
                               start=False, stop=True, tile_position=(0, 64 * half)), ymc)
                if half == 1:
                    t.wait_ge(xhtc, H + 1)
                    t.wait_ge(sup, 32 * k + 2 * o)
                    p.inc(t.matmul(p_S[:, 0:CL], btm_sb[:], xhat[:, (o % 2) * CL:(o % 2 + 1) * CL],
                                   start=True, stop=True), sdm)

        @block.vector
        def _(v, k=k):
            for h in range(NH):
                o, half = divmod(h, 2)
                H = 32 * k + h
                v.wait_ge(hbc, H + 1)
                if H >= 2:
                    v.wait_ge(gxc, H - 1)
                p.inc(v.select(sel_sb[:, (h % 2) * CL:(h % 2 + 1) * CL], triu_sb[:], p_cb[h % 2][:, 0:CL], negt[:]), selc)
                if P5_DBG and k == P5_KD and h == P5_HD:
                    dbg(v, "cb", p_cb[h % 2][:, 0:CL])
                    dbg(v, "sel", sel_sb[:, (h % 2) * CL:(h % 2 + 1) * CL])
                    dbg(v, "cumtm", cum_tm[:, 0:NH])
                    dbg(v, "ncum", ncum_tm[:, 0:NH])
                    dbg(v, "dttm", dt_tm[:, 0:NH])
                    dbg(v, "dtw", dtw_tm[:, 0:NH])
                    dbg(v, "hT", hT_sb[:, 0:CL])
                    dbg(v, "xstm", xs_tm[:, (h // 2) * CL:(h // 2 + 1) * CL])
                v.wait_ge(gxc, H + 1)
                if H >= 2:
                    v.wait_ge(ymc, 2 * (H - 1))
                p.inc(v.tensor_mul(mT[:, (h % 2) * CL:(h % 2 + 1) * CL], gexp[:, (h % 2) * CL:(h % 2 + 1) * CL], hT_sb[:]), mtc)
                if P5_DBG and k == P5_KD and h == P5_HD:
                    dbg(v, "gexp", gexp[:, (h % 2) * CL:(h % 2 + 1) * CL])
                    dbg(v, "mT", mT[:, (h % 2) * CL:(h % 2 + 1) * CL])
                v.wait_ge(ecic, H + 1)
                if H >= 2:
                    v.wait_ge(ymc, 2 * (H - 1))
                p.inc(v.tensor_mul(ctil[:, (h % 2) * CL:(h % 2 + 1) * CL], C_bf[:], eci[:, (h % 2) * CL:(h % 2 + 1) * CL]), ctc)
                if P5_DBG and k == P5_KD and h == P5_HD:
                    dbg(v, "eci", eci[:, (h % 2) * CL:(h % 2 + 1) * CL])
                    dbg(v, "ctil", ctil[:, (h % 2) * CL:(h % 2 + 1) * CL])
                if H >= 2:
                    v.wait_ge(ymc, 2 * (H - 1))
                p.inc(v.tensor_scalar(xdtb[:, (h % 2) * 64:(h % 2 + 1) * 64],
                                      xs_tm[:, o * CL + half * 64:o * CL + half * 64 + 64],
                                      dt_tm[:, h:h + 1], None, ALU.mult), xdtc)
                if k >= 1 or o >= 1:
                    v.wait_ge(sdm, 16 * k + o)
                p.inc(v.tensor_scalar(xhat[:, (o % 2) * CL + half * 64:(o % 2) * CL + half * 64 + 64],
                                      xs_tm[:, o * CL + half * 64:o * CL + half * 64 + 64],
                                      dtw_tm[:, h:h + 1], None, ALU.mult), xhtc)
                v.wait_ge(sup, 32 * k)
                if H >= 2:
                    v.wait_ge(ymc, 2 * (H - 1))
                p.inc(v.tensor_copy(sbf[:, (h % 2) * 64:(h % 2 + 1) * 64], S_nm[:, h * 64:(h + 1) * 64]), sbfc)
                if P5_DBG and k == P5_KD and h == P5_HD:
                    dbg(v, "xdt", xdtb[:, (h % 2) * 64:(h % 2 + 1) * 64])
                    dbg(v, "sbf", sbf[:, (h % 2) * 64:(h % 2 + 1) * 64])
                if half == 1:
                    v.wait_ge(sdm, 16 * k + o + 1)
                    for hf in range(2):
                        hh = 2 * o + hf
                        p.inc(v.scalar_tensor_tensor(S_nm[:, hh * 64:(hh + 1) * 64], S_nm[:, hh * 64:(hh + 1) * 64],
                                                     pc_bc[:, hh:hh + 1], p_S[:, hf * 64:hf * 64 + 64],
                                                     ALU.mult, ALU.add), sup)
                    v.wait_ge(ymc, 64 * k + 4 * o + 4)
                    if P5_DBG and k == P5_KD and o == P5_HD // 2:
                        dbg(v, "yps", p_Y[:, 0:CL])
                    q = 16 * k + o
                    if q >= 2:
                        v.wait_ge(ystp[q % 2], 16 * (q // 2))
                    p.inc(v.scalar_tensor_tensor(y_buf[:, (o % 2) * CL:(o % 2 + 1) * CL], xbcc[:, o * CL:(o + 1) * CL],
                                                 dcol_sb[:, o:o + 1], p_Y[:, 0:CL], ALU.mult, ALU.add), yev)

        @block.scalar
        def _(sc, k=k):
            for h in range(NH):
                H = 32 * k + h
                sc.wait_ge(selc, H + 1)
                sc.wait_ge(nct, k + 1)
                if H >= 2:
                    sc.wait_ge(mtc, H - 1)
                p.inc(sc.activation(gexp[:, (h % 2) * CL:(h % 2 + 1) * CL], sel_sb[:, (h % 2) * CL:(h % 2 + 1) * CL],
                                    AF.Exp, bias=ncum_tm[:, h:h + 1]), gxc)
                if H >= 2:
                    sc.wait_ge(ctc, H - 1)
                p.inc(sc.activation(eci[:, (h % 2) * CL:(h % 2 + 1) * CL], p_cb[h % 2][:, 0:CL], AF.Exp), ecic)

        @block.sync
        def _(s, k=k):
            for o in range(16):
                q = 16 * k + o
                s.wait_ge(yev, q + 1)
                if q >= 2:
                    s.wait_ge(ystp[q % 2], 16 * (q // 2))
                p.incd(s.dma_start(out=y_st[o * 128:(o + 1) * 128, k * CL:(k + 1) * CL],
                                   in_=y_buf[:, (o % 2) * CL:(o % 2 + 1) * CL]), ystp[q % 2])

    @block.sync
    def _(s):
        p.wait(s, sup)
        p.incd(s.dma_start(out=sfin_st[:, :], in_=S_nm[:]), sfd)
        if P5_DBG:
            p.wait(s, dbgc)
            for n, i in _dbg_map.items():
                p.incd(s.dma_start(out=dbg_out[n][:, :], in_=dbg_sb[:, i * 128:(i + 1) * 128]), sfd)
        for sm in [ld, msd, pre, cnv, sil, bcst, sca, wfm_s, trm, evt, pcx, nct,
                   htm, hte, xst, xse, hbc, selc, gxc, ecic, mtc, ctc, xdtc, xhtc,
                   sbfc, ymc, sdm, sup, yev, sfd] + ystp:
            p.wait(s, sm)

    @block.gpsimd
    def _(g):
        p.wait(g, ldg)

    return p





def build_p6():
    p = Prog()
    y_in = p.din("y_in", [D_INNER, TH])
    z_in = p.din("z_in", [D_INNER, TH], BF16)
    xn1_in = p.din("xn1_in", [D, TH], BF16)
    xn1b_in = p.din("xn1b", [D, 1], BF16)
    x1_in = p.din("x1_in", [D, TH])
    woutT_in = p.din("woutT", [D_INNER, D])
    nw_in = p.din("nw", [128, 16])
    wrT_in = p.din("wrT", [D, D])
    wkT_in = p.din("wkT", [D, D])
    wvT_in = p.din("wvT", [D, D])
    mur_in = p.din("mur", [128, 8])
    muk_in = p.din("muk", [128, 8])
    eps_in = p.din("eps11", [1, 1])

    out_st = p.dout("out_st", [D, TH])

    TW = TH + 1
    xn1s = p.sb("xn1s", [128, DC * TW])
    wout_bf = p.sb("wout_bf", [128, 16 * D])
    wr_bf = p.sb("wr_bf", [128, DC * D])
    wk_bf = p.sb("wk_bf", [128, DC * D])
    wv_bf = p.sb("wv_bf", [128, DC * D])
    nw_sb = p.sb("nw_sb", [128, 16], F32)
    mur_sb = p.sb("mur_sb", [128, DC], F32)
    muk_sb = p.sb("muk_sb", [128, DC], F32)
    eps_t = p.sb("eps_t", [1, 1], F32)
    yt = p.sb("yt", [128, 2 * 512], F32)
    zt = p.sb("zt", [128, 2 * 512])
    szt = p.sb("szt", [128, 2 * 512], F32)
    sqt = p.sb("sqt", [128, 2 * 512], F32)
    yg_bf = p.sb("yg_bf", [128, 16 * 512])
    rrow = p.sb("rrow", [1, 512], F32)
    rbc = p.sb("rbc", [128, 512], F32)
    xacc = p.sb("xacc", [128, DC * 512], F32)
    dtile = p.sb("dtile", [128, 512], F32)
    xpf = p.sb("xpf", [128, 512], F32)
    xcf = p.sb("xcf", [128, 512], F32)
    xrb = p.sb("xrb", [128, DC * 512])
    xkb = p.sb("xkb", [128, DC * 512])
    r_f = p.sb("r_f", [128, DC * 512])
    kab = p.sb("kab", [128, DC * 512])
    krel = p.sb("krel", [128, 512], F32)
    x1t = p.sb("x1t", [128, 2 * 512], F32)
    onescol = p.sb("onescol", [128, 1], F32)
    ones_f = p.sb("ones_f", [1, 128], F32)

    p_st = p.ps("p_st", [1, 512])
    p_bc = p.ps("p_bc", [128, 512])
    p_ab = [p.ps("p_a", [128, 512]), p.ps("p_b", [128, 512])]

    ld = p.sem("ld")
    ldg = p.sem("ldg")
    msd = p.sem("msd")
    wfd = p.sem("wfd")
    yzp = [p.sem("yzA"), p.sem("yzB")]
    szd = p.sem("szd")
    ygc = p.sem("ygc")
    sqd = p.sem("sqd")
    smm = p.sem("smm")
    rsc = p.sem("rsc")
    bmm = p.sem("bmm")
    rbe = p.sem("rbe")
    mmo = p.sem("mmo")
    mev = p.sem("mev")
    xrc = p.sem("xrc")
    rmm = p.sem("rmm")
    rse = p.sem("rse")
    kmm = p.sem("kmm")
    kse = p.sem("kse")
    vmm = p.sem("vmm")
    vev = p.sem("vev")
    x1p = [p.sem("x1A"), p.sem("x1B")]
    oad = p.sem("oad")
    ostp = [p.sem("osA"), p.sem("osB")]
    block = p.start()

    @block.sync
    def _(s):
        p.incd(s.dma_start(out=nw_sb[:], in_=nw_in[:, :]), ld)
        p.incd(s.dma_start(out=mur_sb[:], in_=mur_in[:, :]), ld)
        p.incd(s.dma_start(out=muk_sb[:], in_=muk_in[:, :]), ld)
        p.incd(s.dma_start(out=eps_t[:], in_=eps_in[:, :]), ld)
        for i in range(DC):
            p.incd(s.dma_start(out=xn1s[:, i * TW:i * TW + 1], in_=xn1b_in[i * 128:(i + 1) * 128, :]), ld)
            p.incd(s.dma_start(out=xn1s[:, i * TW + 1:(i + 1) * TW], in_=xn1_in[i * 128:(i + 1) * 128, :]), ld)

    @block.gpsimd
    def _(g):
        g.memset(onescol[:], 1.0)
        p.inc(g.memset(ones_f[:], 1.0), msd)
        for i in range(16):
            p.incd(g.dma_start(out=wout_bf[:, i * D:(i + 1) * D], in_=woutT_in[i * 128:(i + 1) * 128, :]), ldg)
        for wsb, wi in ((wr_bf, wrT_in), (wk_bf, wkT_in), (wv_bf, wvT_in)):
            for i in range(DC):
                p.incd(g.dma_start(out=wsb[:, i * D:(i + 1) * D], in_=wi[i * 128:(i + 1) * 128, :]), ldg)
        p.wait(g, ld)
        g.wait_ge(ldg, 16 * 16)
        for i in range(16):
            ins = g.tensor_scalar_mul(wout_bf[:, i * D:(i + 1) * D], wout_bf[:, i * D:(i + 1) * D],
                                      nw_sb[:, i:i + 1])
        p.inc(ins, wfd)

    for m in range(TMH):
        cs = slice(m * 512, (m + 1) * 512)

        @block.sync
        def _(s, m=m):
            for ic in range(16):
                n = 16 * m + ic
                if n >= 2:
                    s.wait_ge(ygc, n - 1)
                p.incd(s.dma_start(out=yt[:, (n % 2) * 512:(n % 2 + 1) * 512],
                                   in_=y_in[ic * 128:(ic + 1) * 128, cs]), yzp[n % 2])
                p.incd(s.dma_start(out=zt[:, (n % 2) * 512:(n % 2 + 1) * 512],
                                   in_=z_in[ic * 128:(ic + 1) * 128, cs]), yzp[n % 2])

        @block.scalar
        def _(sc, m=m):
            for ic in range(16):
                n = 16 * m + ic
                sc.wait_ge(yzp[n % 2], 32 * (n // 2 + 1))
                if n >= 2:
                    sc.wait_ge(ygc, n - 1)
                p.inc(sc.activation(szt[:, (n % 2) * 512:(n % 2 + 1) * 512],
                                    zt[:, (n % 2) * 512:(n % 2 + 1) * 512], AF.Silu), szd)

        @block.vector
        def _(v, m=m):
            for ic in range(16):
                n = 16 * m + ic
                v.wait_ge(szd, n + 1)
                if m >= 1:
                    v.wait_ge(mmo, 8 * m)
                p.inc(v.tensor_mul(yg_bf[:, ic * 512:(ic + 1) * 512],
                                   yt[:, (n % 2) * 512:(n % 2 + 1) * 512],
                                   szt[:, (n % 2) * 512:(n % 2 + 1) * 512]), ygc)

        @block.scalar
        def _(sc, m=m):
            for ic in range(16):
                n = 16 * m + ic
                sc.wait_ge(ygc, n + 1)
                if n >= 2:
                    sc.wait_ge(smm, n - 1)
                p.inc(sc.activation(sqt[:, (n % 2) * 512:(n % 2 + 1) * 512],
                                    yg_bf[:, ic * 512:(ic + 1) * 512], AF.Square), sqd)

        @block.tensor
        def _(t, m=m):
            if m == 0:
                p.wait(t, msd)
            for ic in range(16):
                n = 16 * m + ic
                t.wait_ge(sqd, n + 1)
                if ic == 0:
                    t.wait_ge(rsc, 3 * m)
                p.inc(t.matmul(p_st[:], onescol[:], sqt[:, (n % 2) * 512:(n % 2 + 1) * 512],
                               start=(ic == 0), stop=(ic == 15)), smm)

        @block.scalar
        def _(sc, m=m):
            sc.wait_ge(smm, 16 * (m + 1))
            if m == 0:
                p.wait(sc, ld)
            sc.activation(rrow[:], p_st[:], AF.Copy, scale=1.0 / float(D_INNER))
            sc.activation(rrow[:], rrow[:], AF.Ln, bias=eps_t[:])
            p.inc(sc.activation(rrow[:], rrow[:], AF.Exp, scale=-0.5), rsc, 3)

        @block.tensor
        def _(t, m=m):
            t.wait_ge(rsc, 3 * (m + 1))
            if m >= 1:
                t.wait_ge(rbe, m)
            p.inc(t.matmul(p_bc[:], ones_f[0:1, :], rrow[0:1, :], start=True, stop=True), bmm)

        @block.vector
        def _(v, m=m):
            v.wait_ge(bmm, m + 1)
            if m >= 1:
                v.wait_ge(mev, 8 * m)
            p.inc(v.tensor_copy(rbc[:], p_bc[:]), rbe)

        @block.tensor
        def _(t, m=m):
            if m == 0:
                p.wait(t, wfd)
            for oc in range(DC):
                t.wait_ge(ygc, 16 * (m + 1))
                if oc >= 2:
                    t.wait_ge(mev, 8 * m + oc - 1)
                for ic in range(16):
                    ins = t.matmul(p_ab[oc % 2][:], wout_bf[:, ic * D + oc * 128:ic * D + (oc + 1) * 128],
                                   yg_bf[:, ic * 512:(ic + 1) * 512], start=(ic == 0), stop=(ic == 15))
                p.inc(ins, mmo)

        @block.vector
        def _(v, m=m):
            for oc in range(DC):
                v.wait_ge(mmo, 8 * m + oc + 1)
                v.wait_ge(rbe, m + 1)
                q = 8 * m + oc
                if q >= 8:
                    v.wait_ge(ostp[q % 2], 16 * ((q - 8) // 2 + 1))
                p.inc(v.tensor_mul(xacc[:, oc * 512:(oc + 1) * 512], p_ab[oc % 2][:], rbc[:]), mev)

        @block.vector
        def _(v, m=m):
            if m >= 1:
                v.wait_ge(rmm, 8 * m)
                v.wait_ge(kmm, 8 * m)
            for c8 in range(DC):
                xc = xn1s[:, c8 * TW + 1 + m * 512: c8 * TW + 1 + (m + 1) * 512]
                xp = xn1s[:, c8 * TW + m * 512: c8 * TW + (m + 1) * 512]
                v.tensor_copy(xpf[:], xp)
                v.tensor_copy(xcf[:], xc)
                v.tensor_sub(dtile[:], xcf[:], xpf[:])
                v.scalar_tensor_tensor(xrb[:, c8 * 512:(c8 + 1) * 512], dtile[:], mur_sb[:, c8:c8 + 1],
                                       xpf[:], ALU.mult, ALU.add)
                p.inc(v.scalar_tensor_tensor(xkb[:, c8 * 512:(c8 + 1) * 512], dtile[:], muk_sb[:, c8:c8 + 1],
                                             xpf[:], ALU.mult, ALU.add), xrc)

        @block.tensor
        def _(t, m=m):
            for oc in range(DC):
                t.wait_ge(xrc, 8 * (m + 1))
                if oc >= 2:
                    t.wait_ge(rse, 8 * m + oc - 1)
                else:
                    t.wait_ge(mev, 8 * m + 6 + oc + 1)
                for c8 in range(DC):
                    ins = t.matmul(p_ab[oc % 2][:], wr_bf[:, c8 * D + oc * 128:c8 * D + (oc + 1) * 128],
                                   xrb[:, c8 * 512:(c8 + 1) * 512], start=(c8 == 0), stop=(c8 == DC - 1))
                p.inc(ins, rmm)

        @block.scalar
        def _(sc, m=m):
            for oc in range(DC):
                sc.wait_ge(rmm, 8 * m + oc + 1)
                if m >= 1:
                    sc.wait_ge(vev, 8 * m)
                p.inc(sc.activation(r_f[:, oc * 512:(oc + 1) * 512], p_ab[oc % 2][:], AF.Sigmoid), rse)

        @block.tensor
        def _(t, m=m):
            for oc in range(DC):
                if oc >= 2:
                    t.wait_ge(kse, 2 * (8 * m + oc - 1))
                else:
                    t.wait_ge(rse, 8 * m + 6 + oc + 1)
                for c8 in range(DC):
                    ins = t.matmul(p_ab[oc % 2][:], wk_bf[:, c8 * D + oc * 128:c8 * D + (oc + 1) * 128],
                                   xkb[:, c8 * 512:(c8 + 1) * 512], start=(c8 == 0), stop=(c8 == DC - 1))
                p.inc(ins, kmm)

        @block.scalar
        def _(sc, m=m):
            for oc in range(DC):
                sc.wait_ge(kmm, 8 * m + oc + 1)
                if oc >= 2:
                    sc.wait_ge(kse, 2 * (8 * m + oc) - 2)
                if m >= 1:
                    sc.wait_ge(vmm, 8 * m)
                sc.activation(krel[:], p_ab[oc % 2][:], AF.Relu)
                p.inc(sc.activation(kab[:, oc * 512:(oc + 1) * 512], krel[:], AF.Square), kse, 2)

        @block.tensor
        def _(t, m=m):
            for oc in range(DC):
                t.wait_ge(kse, 2 * (8 * m + 8))
                if oc >= 2:
                    t.wait_ge(vev, 8 * m + oc - 1)
                else:
                    t.wait_ge(kse, 2 * (8 * m + 6 + oc + 1))
                for c8 in range(DC):
                    ins = t.matmul(p_ab[oc % 2][:], wv_bf[:, c8 * D + oc * 128:c8 * D + (oc + 1) * 128],
                                   kab[:, c8 * 512:(c8 + 1) * 512], start=(c8 == 0), stop=(c8 == DC - 1))
                p.inc(ins, vmm)

        @block.vector
        def _(v, m=m):
            for oc in range(DC):
                v.wait_ge(vmm, 8 * m + oc + 1)
                v.wait_ge(rse, 8 * m + oc + 1)
                ff = dtile[:]
                if oc >= 2:
                    v.wait_ge(vev, 8 * m + oc - 1)
                v.tensor_mul(ff, r_f[:, oc * 512:(oc + 1) * 512], p_ab[oc % 2][:])
                p.inc(v.tensor_add(xacc[:, oc * 512:(oc + 1) * 512], xacc[:, oc * 512:(oc + 1) * 512], ff), vev)

        @block.sync
        def _(s, m=m):
            for oc in range(DC):
                q = 8 * m + oc
                if q >= 2:
                    s.wait_ge(oad, q - 1)
                p.incd(s.dma_start(out=x1t[:, (q % 2) * 512:(q % 2 + 1) * 512],
                                   in_=x1_in[oc * 128:(oc + 1) * 128, cs]), x1p[q % 2])

        @block.vector
        def _(v, m=m):
            for oc in range(DC):
                q = 8 * m + oc
                v.wait_ge(x1p[q % 2], 16 * (q // 2 + 1))
                v.wait_ge(vev, 8 * m + oc + 1)
                xc = xn1s[:, oc * TW + 1 + m * 512: oc * TW + 1 + (m + 1) * 512]
                v.tensor_copy(xcf[:], xc)
                v.tensor_add(xacc[:, oc * 512:(oc + 1) * 512], xacc[:, oc * 512:(oc + 1) * 512], xcf[:])
                p.inc(v.tensor_add(xacc[:, oc * 512:(oc + 1) * 512], xacc[:, oc * 512:(oc + 1) * 512],
                                   x1t[:, (q % 2) * 512:(q % 2 + 1) * 512]), oad)

        @block.sync
        def _(s, m=m):
            for oc in range(DC):
                q = 8 * m + oc
                s.wait_ge(oad, q + 1)
                if q >= 2:
                    s.wait_ge(ostp[q % 2], 16 * (q // 2))
                p.incd(s.dma_start(out=out_st[oc * 128:(oc + 1) * 128, cs],
                                   in_=xacc[:, oc * 512:(oc + 1) * 512]), ostp[q % 2])

    @block.sync
    def _(s):
        for sm in [ld, ldg, msd, wfd, szd, ygc, sqd, smm, rsc, bmm, rbe, mmo,
                   mev, xrc, rmm, rse, kmm, kse, vmm, vev, oad] + yzp + x1p + ostp:
            p.wait(s, sm)

    return p



def _p1_maps(inp):
    x = np.asarray(inp["L0_attn_x"])
    Wqkv = np.asarray(inp["L0_attn_W_qkv"])
    gam8 = np.ascontiguousarray(np.asarray(inp["L0_attn_ln_gamma"]).reshape(8, 128))
    bet8 = np.ascontiguousarray(np.asarray(inp["L0_attn_ln_beta"]).reshape(8, 128))
    eps11 = np.full((1, 1), EPS, np.float32)
    maps = []
    for c in range(NCORES):
        b, g = divmod(c, 2)
        heads = range(8 * g, 8 * g + 8)
        qrows = np.concatenate([np.arange(192 * h, 192 * h + 64) for h in heads])
        m = dict(
            x_fm=np.ascontiguousarray(x[b].T),
            wqT=np.ascontiguousarray(Wqkv[qrows].T),
            wkT=np.ascontiguousarray(Wqkv[qrows + 64].T),
            wvT=np.ascontiguousarray(Wqkv[qrows + 128].T),
            gam8=gam8, bet8=bet8, eps11=eps11,
        )
        maps.append(m)
    return maps


_PROGS = {}


def _prog(name, builder):
    if name not in _PROGS:
        _PROGS[name] = builder()
    return _PROGS[name]


def kernel(**inputs):
    import ml_dtypes
    inp = inputs
    eps11 = np.full((1, 1), EPS, np.float32)
    ident = np.eye(128, dtype=np.float32)
    id_bf = ident.astype(ml_dtypes.bfloat16)
    triq = np.tril(np.ones((128, 128), np.uint8))
    trik = np.triu(np.ones((128, 128), np.uint8))

    r1 = _prog("p1", build_p1).run(_p1_maps(inp))

    maps2 = [dict(q_st=r1.results[c]["q_st"], k_st=r1.results[c]["k_st"],
                  v_st=r1.results[c]["v_st"], triq=triq, trik=trik, ident_f=ident)
             for c in range(NCORES)]
    r2 = _prog("p2", build_p2).run(maps2)

    xmax_parts = np.concatenate([np.asarray(r1.results[c]["xmax_st"]) for c in range(NCORES)], 0)
    WoT = np.ascontiguousarray(np.asarray(inp["L0_attn_W_o"], np.float32).T)
    maps3 = []
    for c in range(NCORES):
        b, g = divmod(c, 2)
        sl = slice(TH * g, TH * (g + 1))
        xn_full = np.asarray(r1.results[c]["xn_st"])
        ctx0 = np.asarray(r2.results[2 * b]["ctx_st"])
        ctx1 = np.asarray(r2.results[2 * b + 1]["ctx_st"])
        maps3.append(dict(
            xn_own=np.ascontiguousarray(xn_full[:, sl]),
            ctx_own=np.ascontiguousarray(np.concatenate([ctx0[:, sl], ctx1[:, sl]], 0)),
            xmax_parts=xmax_parts,
            clip_x=np.asarray(inp["L0_ffn_clip_x"], np.float32).reshape(1, 1),
            w_gate=np.asarray(inp["L0_ffn_w_gate"], np.float32),
            w_up=np.asarray(inp["L0_ffn_w_up"], np.float32),
            W_oT=WoT, ident_f=ident,
        ))
    r3 = _prog("p3", build_p3).run(maps3)

    hmax_parts = np.concatenate([np.asarray(r3.results[c]["hmax_st"]) for c in range(NCORES)], 0)
    maps4a1 = [dict(h_st=r3.results[c]["h_st"], hmax_parts=hmax_parts,
                    clip_down=np.asarray(inp["L0_ffn_clip_down"], np.float32).reshape(1, 1),
                    w_down=np.asarray(inp["L0_ffn_w_down"], np.float32), ident_f=ident)
               for c in range(NCORES)]
    r4a1 = _prog("p4a1", build_p4a1).run(maps4a1)

    x = np.asarray(inp["L0_attn_x"], np.float32)
    maps4a2 = []
    for c in range(NCORES):
        b, g = divmod(c, 2)
        sl = slice(TH * g, TH * (g + 1))
        maps4a2.append(dict(
            x_own=np.ascontiguousarray(x[b, sl].T),
            attn_st=r3.results[c]["attn_st"],
            ffn0_st=r4a1.results[c]["ffn0_st"],
            eps11=eps11,
        ))
    r4a2 = _prog("p4a2", build_p4a2).run(maps4a2)

    Win = np.asarray(inp["L1_attn_in_proj_w"], np.float32)
    WinT = np.zeros((D, 4480), np.float32)
    WinT[:, 0:4384] = Win.T
    maps4b = [dict(xn1_own=r4a2.results[c]["xn1_st"], W_inT=WinT) for c in range(NCORES)]
    r4b = _prog("p4b", build_p4b).run(maps4b)

    cw = np.asarray(inp["L1_attn_conv_w"], np.float32)
    cb = np.asarray(inp["L1_attn_conv_b"], np.float32)
    Dv = np.asarray(inp["L1_attn_D"], np.float32)
    esel = np.zeros((NH, NH * CL), np.float32)
    for h in range(NH):
        esel[h, h * CL:(h + 1) * CL] = 1.0
    d_col = np.zeros((128, NTC), np.float32)
    for o in range(NTC):
        d_col[0:64, o] = Dv[2 * o]
        d_col[64:128, o] = Dv[2 * o + 1]
    zero_tail = np.zeros((2304, 3), ml_dtypes.bfloat16)
    zero_s0 = np.zeros((128, D_INNER), np.float32)

    def p5_maps(s0_list):
        maps = []
        for c in range(NCORES):
            b, g = divmod(c, 2)
            if g == 0:
                tail = zero_tail
            else:
                xbc_even = np.asarray(r4b.results[c - 1]["xbc_st"])
                tail = np.ascontiguousarray(xbc_even[:, TH - 3:TH])
            maps.append(dict(
                xbc_own=r4b.results[c]["xbc_st"],
                xbc_tail=tail,
                dtr_in=r4b.results[c]["dt_st"],
                s0_in=s0_list[c],
                conv_w=np.ascontiguousarray(cw.reshape(NCH, 128, DCONV).transpose(1, 0, 2).reshape(128, NCH * DCONV)),
                conv_b=np.ascontiguousarray(cb.reshape(NCH, 128).T),
                dt_bias=np.asarray(inp["L1_attn_dt_bias"], np.float32).reshape(NH, 1),
                a_log=np.asarray(inp["L1_attn_A_log"], np.float32).reshape(NH, 1),
                d_col=d_col, esel=esel,
                triu_m=trik, ident_f=ident, id_bf=id_bf,
            ))
        return maps

    p5 = _prog("p5", build_p5)
    r5a = p5.run(p5_maps([zero_s0] * NCORES))
    s0b = [zero_s0 if c % 2 == 0 else np.asarray(r5a.results[c - 1]["sfin_st"])
           for c in range(NCORES)]
    r5 = p5.run(p5_maps(s0b))

    nw = np.asarray(inp["L1_attn_norm_w"], np.float32)
    Wout = np.asarray(inp["L1_attn_out_proj_w"], np.float32)
    maps6 = []
    for c in range(NCORES):
        b, g = divmod(c, 2)
        if g == 0:
            xb = np.zeros((D, 1), ml_dtypes.bfloat16)
        else:
            xn1_even = np.asarray(r4a2.results[c - 1]["xn1_st"])
            xb = np.ascontiguousarray(xn1_even[:, TH - 1:TH])
        maps6.append(dict(
            y_in=r5.results[c]["y_st"],
            z_in=r4b.results[c]["z_st"],
            xn1_in=r4a2.results[c]["xn1_st"],
            xn1b=xb,
            x1_in=r4a2.results[c]["x1_st"],
            woutT=np.ascontiguousarray(Wout.T),
            nw=np.ascontiguousarray(nw.reshape(16, 128).T),
            wrT=np.ascontiguousarray(np.asarray(inp["L1_ffn_W_r"], np.float32).T),
            wkT=np.ascontiguousarray(np.asarray(inp["L1_ffn_W_k"], np.float32).T),
            wvT=np.ascontiguousarray(np.asarray(inp["L1_ffn_W_v"], np.float32).T),
            mur=np.ascontiguousarray(np.asarray(inp["L1_ffn_mu_r"], np.float32).reshape(8, 128).T),
            muk=np.ascontiguousarray(np.asarray(inp["L1_ffn_mu_k"], np.float32).reshape(8, 128).T),
            eps11=eps11,
        ))
    r6 = _prog("p6", build_p6).run(maps6)

    out = np.zeros((B, T, D), np.float32)
    for c in range(NCORES):
        b, g = divmod(c, 2)
        sl = slice(TH * g, TH * (g + 1))
        out[b, sl, :] = np.asarray(r6.results[c]["out_st"], np.float32).T
    return out


if __name__ == "__main__":
    print("import ok")
